# version 14
# baseline (speedup 1.0000x reference)
import numpy as np
import ml_dtypes
import concourse.bass as bass
import concourse.mybir as mybir
from concourse.bass_utils import run_bass_kernel_spmd

F32 = mybir.dt.float32
BF16 = mybir.dt.bfloat16
I32 = mybir.dt.int32
AF = mybir.ActivationFunctionType
ALU = mybir.AluOpType

D = 1024
H = 16
HD = 64
DH = 3072
KC = D // 128
LN_EPS = 1e-5
QS = 1.0 / 127.0
MAGIC = 0x5F3759DF


class Plan:
    def __init__(self):
        self.ops = []
        self.cnt = {}

    def op(self, eng, fn, waits=(), inc=None):
        rec = None
        if inc is not None:
            sem, n = inc if isinstance(inc, tuple) else (inc, 1)
            self.cnt[sem] = self.cnt.get(sem, 0) + n
            rec = (sem, n)
        ws = [w for w in waits if w is not None and w[1] > 0]
        self.ops.append((eng, fn, ws, rec))
        return (rec[0], self.cnt[rec[0]]) if rec else None


def _build(L, GROUP, SLICE, dbg=None):
    CH = L // 128
    NG = L // GROUP
    CPG = GROUP // 128
    NS = DH // SLICE
    JS = SLICE // 128

    nc = bass.Bass("TRN2", target_bir_lowering=False)
    x_d = nc.dram_tensor("x", [L, D], F32, kind="ExternalInput").ap()
    xT_d = nc.dram_tensor("xT", [D, L], F32, kind="ExternalInput").ap()
    wqT_d = nc.dram_tensor("wqT", [D, D], F32, kind="ExternalInput").ap()
    wvT_d = nc.dram_tensor("wvT", [D, D], F32, kind="ExternalInput").ap()
    wgT_d = nc.dram_tensor("wgT", [D, D], F32, kind="ExternalInput").ap()
    woT_d = nc.dram_tensor("woT", [D, D], F32, kind="ExternalInput").ap()
    dec_d = nc.dram_tensor("dec", [H], F32, kind="ExternalInput").ap()
    gam_d = nc.dram_tensor("gam", [D], F32, kind="ExternalInput").ap()
    bet_d = nc.dram_tensor("bet", [D], F32, kind="ExternalInput").ap()
    wgq_d = nc.dram_tensor("wgq", [D, DH], I32, kind="ExternalInput").ap()
    wuq_d = nc.dram_tensor("wuq", [D, DH], I32, kind="ExternalInput").ap()
    wdq_d = nc.dram_tensor("wdq", [DH, D], I32, kind="ExternalInput").ap()
    y_d = nc.dram_tensor("y", [L, D], F32, kind="ExternalOutput").ap()
    wgb_d = nc.dram_tensor("wgb", [D, DH], BF16, kind="Internal").ap()
    wub_d = nc.dram_tensor("wub", [D, DH], BF16, kind="Internal").ap()
    wdb_d = nc.dram_tensor("wdb", [DH, D], BF16, kind="Internal").ap()
    if dbg in ("probe", "ffn2"):
        dbg_d = {nm: nc.dram_tensor(nm, [L, D], BF16, kind="ExternalOutput").ap()
                 for nm in ("dq", "dv", "dy", "dgs", "dyy", "dy2f")}
        dbg_d["dx2"] = nc.dram_tensor("dx2", [L, D], F32, kind="ExternalOutput").ap()
        dbg_d["dro"] = nc.dram_tensor("dro", [L, D], F32, kind="ExternalOutput").ap()
        dtm_d = nc.dram_tensor("dtm", [128, H * 128], BF16, kind="ExternalOutput").ap()
        dsum_d = nc.dram_tensor("dsum", [128, 96], F32, kind="ExternalOutput").ap()
        dcr_d = nc.dram_tensor("dcr", [CH * 16, 1024], F32, kind="ExternalOutput").ap()
    if dbg == "ffn2":
        dst_d = nc.dram_tensor("dstate", [L, D], F32, kind="ExternalOutput").ap()
        dst2_d = nc.dram_tensor("dstate2", [L, D], F32, kind="ExternalOutput").ap()
        dr2_d = nc.dram_tensor("dr2", [L, D], F32, kind="ExternalOutput").ap()
        dxf_d = nc.dram_tensor("dxf", [NG * 128, KC * GROUP], BF16,
                               kind="ExternalOutput").ap()
        dhb_d = nc.dram_tensor("dhb", [NG * NS * (GROUP // 512) * 128, 1024], BF16,
                               kind="ExternalOutput").ap()
        dgs2_d = nc.dram_tensor("dgs2", [NG * NS * (GROUP // 512) * 128, 1024], BF16,
                                kind="ExternalOutput").ap()
        dpa_d = nc.dram_tensor("dpa", [128, 1024], F32, kind="ExternalOutput").ap()
        dwg_d = nc.dram_tensor("dwg", [128, KC * SLICE], BF16, kind="ExternalOutput").ap()

    P = Plan()

    from contextlib import ExitStack
    with ExitStack() as _es:
        wq_bf = _es.enter_context(nc.sbuf_tensor("wq_bf", [128, KC * D], BF16))
        wv_bf = _es.enter_context(nc.sbuf_tensor("wv_bf", [128, KC * D], BF16))
        wg_bf = _es.enter_context(nc.sbuf_tensor("wg_bf", [128, KC * D], BF16))
        wo_bf = _es.enter_context(nc.sbuf_tensor("wo_bf", [128, KC * D], BF16))
        tmat = _es.enter_context(nc.sbuf_tensor("tmat", [128, H * 128], BF16))
        mask16 = _es.enter_context(nc.sbuf_tensor("mask16", [16, 1024], BF16))
        av16f = _es.enter_context(nc.sbuf_tensor("av16f", [16, 128], F32))

        cmat = _es.enter_context(nc.sbuf_tensor("cmat", [128, 16], BF16))
        cmatf = _es.enter_context(nc.sbuf_tensor("cmatf", [128, 16], F32))
        l128d = _es.enter_context(nc.sbuf_tensor("l128d", [16, 16], F32))
        lnl_c = _es.enter_context(nc.sbuf_tensor("lnl_c", [16, 1], F32))
        cr16f = _es.enter_context(nc.sbuf_tensor("cr16f", [16, 1024], F32))

        gam_bc = _es.enter_context(nc.sbuf_tensor("gam_bc", [128, D], BF16))
        bet_bc = _es.enter_context(nc.sbuf_tensor("bet_bc", [128, D], BF16))
        ident = _es.enter_context(nc.sbuf_tensor("ident", [128, 128], BF16))
        carr_f = _es.enter_context(nc.sbuf_tensor("carr_f", [1, D], F32))
        sums = _es.enter_context(nc.sbuf_tensor("sums", [128, 96], F32))
        scr_bf = _es.enter_context(nc.sbuf_tensor("scr_bf", [128, D], BF16))
        dec_f = _es.enter_context(nc.sbuf_tensor("dec_f", [1, 16], F32))
        lnl_f = _es.enter_context(nc.sbuf_tensor("lnl_f", [1, 16], F32))
        ones1 = _es.enter_context(nc.sbuf_tensor("ones1", [1, 128], F32))
        e128 = _es.enter_context(nc.sbuf_tensor("e128", [1, 128], F32))
        lnlbc = _es.enter_context(nc.sbuf_tensor("lnlbc", [128, 16], F32))
        E_f = _es.enter_context(nc.sbuf_tensor("E_f", [128, 128], F32))
        Ttmp0 = _es.enter_context(nc.sbuf_tensor("Ttmp0", [128, 128], F32))
        Ttmp1 = Ttmp0
        idf = Ttmp0
        stage = _es.enter_context(nc.sbuf_tensor("stage", [128, D], F32))
        xf_bf0 = _es.enter_context(nc.sbuf_tensor("xf_bf0", [128, D], BF16))
        xf_bf1 = _es.enter_context(nc.sbuf_tensor("xf_bf1", [128, D], BF16))
        xtok0 = _es.enter_context(nc.sbuf_tensor("xtok0", [128, D], BF16))
        xtok1 = _es.enter_context(nc.sbuf_tensor("xtok1", [128, D], BF16))
        qbf0 = _es.enter_context(nc.sbuf_tensor("qbf0", [128, D], BF16))
        qbf1 = _es.enter_context(nc.sbuf_tensor("qbf1", [128, D], BF16))
        vbf0 = _es.enter_context(nc.sbuf_tensor("vbf0", [128, D], BF16))
        vbf1 = _es.enter_context(nc.sbuf_tensor("vbf1", [128, D], BF16))
        gsb0 = _es.enter_context(nc.sbuf_tensor("gsb0", [128, D], BF16))
        gsb1 = _es.enter_context(nc.sbuf_tensor("gsb1", [128, D], BF16))
        ybf0 = _es.enter_context(nc.sbuf_tensor("ybf0", [128, D], BF16))
        ybf1 = _es.enter_context(nc.sbuf_tensor("ybf1", [128, D], BF16))
        y2f0 = _es.enter_context(nc.sbuf_tensor("y2f0", [128, D], BF16))
        y2f1 = _es.enter_context(nc.sbuf_tensor("y2f1", [128, D], BF16))
        x2bf0 = _es.enter_context(nc.sbuf_tensor("x2bf0", [128, D], BF16))
        x2bf1 = _es.enter_context(nc.sbuf_tensor("x2bf1", [128, D], BF16))
        x2f = _es.enter_context(nc.sbuf_tensor("x2f", [128, KC, GROUP], BF16))
        facc = _es.enter_context(nc.sbuf_tensor("facc", [128, CPG, D], F32))
        wgs0 = _es.enter_context(nc.sbuf_tensor("wgs0", [128, KC * SLICE], BF16))
        wgs1 = _es.enter_context(nc.sbuf_tensor("wgs1", [128, KC * SLICE], BF16))
        wus0 = _es.enter_context(nc.sbuf_tensor("wus0", [128, KC * SLICE], BF16))
        wus1 = _es.enter_context(nc.sbuf_tensor("wus1", [128, KC * SLICE], BF16))
        wds0 = _es.enter_context(nc.sbuf_tensor("wds0", [128, JS * D], BF16))
        wds1 = _es.enter_context(nc.sbuf_tensor("wds1", [128, JS * D], BF16))
        cvt32a = _es.enter_context(nc.sbuf_tensor("cvt32a", [128, 512], I32))
        cvt32b = _es.enter_context(nc.sbuf_tensor("cvt32b", [128, 512], I32))
        cvtbfa = _es.enter_context(nc.sbuf_tensor("cvtbfa", [128, 512], BF16))
        cvtbfb = _es.enter_context(nc.sbuf_tensor("cvtbfb", [128, 512], BF16))
        gs20 = _es.enter_context(nc.sbuf_tensor("gs20", [128, 1024], BF16))
        gs21 = gs20
        hbf0 = _es.enter_context(nc.sbuf_tensor("hbf0", [128, 1024], BF16))
        hbf1 = _es.enter_context(nc.sbuf_tensor("hbf1", [128, 1024], BF16))
        PA = _es.enter_context(nc.psum_tensor("PA", [128, 1024], F32))
        PB = _es.enter_context(nc.psum_tensor("PB", [128, 1024], F32))
        PC = _es.enter_context(nc.psum_tensor("PC", [128, 1024], F32))
        PD = _es.enter_context(nc.psum_tensor("PD", [128, 1024], F32))
        semT = _es.enter_context(nc.semaphore("semT"))
        semV = _es.enter_context(nc.semaphore("semV"))
        semS = _es.enter_context(nc.semaphore("semS"))
        semG = _es.enter_context(nc.semaphore("semG"))
        DCH = ["stage", "carr", "dec", "wgs0", "wgs1", "wus0", "wus1",
               "wds0", "wds1", "cvt0", "cvt1", "cvs0", "cvs1"] + \
              ["fac%d" % i for i in range(8)]
        DSEM = {c: _es.enter_context(nc.semaphore("semD" + c)) for c in DCH}
        semF = _es.enter_context(nc.semaphore("semF"))
        semO = _es.enter_context(nc.semaphore("semO"))
        block = _es.enter_context(nc.Block())
        SEM = dict(T=semT, V=semV, S=semS, G=semG, F=semF, O=semO)
        for c in DCH:
            SEM["D" + c] = DSEM[c]
        xf_bf = [xf_bf0, xf_bf1]
        xtok = [xtok0, xtok1]
        qbf = [qbf0, qbf1]
        vbf = [vbf0, vbf1]
        gsb = [gsb0, gsb1]
        ybf = [ybf0, ybf1]
        yy = ybf
        y2f = [y2f0, y2f1]
        x2bf = [x2bf0, x2bf1]
        wgs = [wgs0, wgs1]
        wus = [wus0, wus1]
        wds = [wds0, wds1]
        cvt32 = [cvt32a, cvt32b]
        cvtbf = [cvtbfa, cvtbfb]
        gs2 = [gs20, gs21]
        hbf = [hbf0, hbf1]


        def sl(i):
            return sums[:, i : i + 1]

        def ydma(fn, waits=(), ch="stage"):
            return P.op("Y", fn, waits, inc=("D" + ch, 16))

        def ydma_flush():
            pass

        def vnop():
            P.op("V", lambda v: v.engine_nop())

        def vchain(fn, waits=()):
            return P.op("V", fn, waits, inc="V")

        def nr_rsqrt(veps_i, rstd_i, tmp_i, ev0):
            ve = sl(veps_i)
            rs = sl(rstd_i)
            tm = sl(tmp_i)
            vh = sl(tmp_i + 1)
            e1 = vchain(lambda v, ve=ve, tm=tm: v.tensor_scalar(
                tm.bitcast(I32), ve.bitcast(I32), 1, None, ALU.arith_shift_right), [ev0])
            vchain(lambda v, ve=ve, vh=vh: v.tensor_scalar(vh, ve, -0.5, None, ALU.mult), [ev0])
            e = vchain(lambda v, rs=rs, tm=tm: v.tensor_scalar(
                rs.bitcast(I32), tm.bitcast(I32), MAGIC, -1, ALU.subtract, ALU.mult), [e1])
            for _ in range(1):
                e = vchain(lambda v, rs=rs, tm=tm: v.tensor_tensor(tm, rs, rs, ALU.mult), [e])
                e = vchain(lambda v, tm=tm, vh=vh: v.tensor_scalar(
                    tm, tm, vh, 1.5, ALU.mult, ALU.add), [e])
                e = vchain(lambda v, rs=rs, tm=tm: v.tensor_tensor(rs, rs, tm, ALU.mult), [e])
            return e

        e_dec = ydma(lambda s: s.dma_start(out=dec_f[0:1, 0:16], in_=dec_d[None, :]),
                     ch="dec")
        e_gam = ydma(lambda s: s.dma_start(out=carr_f[0:1, :], in_=gam_d[None, :]),
                     ch="carr")

        ge = None
        def gop(fn):
            nonlocal_ns["ge"] = P.op("G", fn, [nonlocal_ns["ge"]], inc="G")
        nonlocal_ns = {"ge": None}
        gop(lambda g: g.iota(E_f[:], pattern=[[1, 128]], base=0,
                             channel_multiplier=-1, allow_small_or_imprecise_dtypes=True))
        gop(lambda g: g.affine_select(E_f[:], E_f[:], pattern=[[1, 128]],
                                      compare_op=ALU.is_ge, fill=1e30, base=0,
                                      channel_multiplier=-1))
        gop(lambda g: g.memset(idf[:], 1.0))
        gop(lambda g: g.affine_select(idf[:], idf[:], pattern=[[1, 128]],
                                      compare_op=ALU.is_equal, fill=0.0, base=0,
                                      channel_multiplier=-1))
        gop(lambda g: g.memset(ones1[:], 1.0))
        gop(lambda g: g.iota(e128[0:1, :], pattern=[[1, 128]], base=1,
                             channel_multiplier=0,
                             allow_small_or_imprecise_dtypes=True))
        gop(lambda g: g.memset(mask16[:], 1.0))
        gop(lambda g: g.affine_select(mask16[:], mask16[:], pattern=[[1, 1024]],
                                      compare_op=ALU.is_ge, fill=0.0, base=0,
                                      channel_multiplier=-64))
        gop(lambda g: g.affine_select(mask16[:], mask16[:], pattern=[[-1, 1024]],
                                      compare_op=ALU.is_ge, fill=0.0, base=63,
                                      channel_multiplier=64))
        gop(lambda g: g.iota(av16f[:], pattern=[[1, 128]], base=1,
                             channel_multiplier=0,
                             allow_small_or_imprecise_dtypes=True))
        gop(lambda g: g.iota(cmatf[:], pattern=[[0, 16]], base=127,
                             channel_multiplier=-1,
                             allow_small_or_imprecise_dtypes=True))
        gop(lambda g: g.memset(l128d[:], 1.0))
        gop(lambda g: g.affine_select(l128d[:], l128d[:], pattern=[[1, 16]],
                                      compare_op=ALU.is_equal, fill=0.0, base=0,
                                      channel_multiplier=-1))
        gop(lambda g: g.memset(cr16f[:], 0.0))
        e_gp = P.op("G", lambda g: g.nop(), [nonlocal_ns["ge"]], inc="G")

        e_id = P.op("S", lambda s: s.copy(ident[:], idf[:]), [e_gp], inc="S")
        e_lnl = P.op("S", lambda s: s.activation(lnl_f[0:1, 0:16], dec_f[0:1, 0:16], AF.Ln),
                     [e_dec], inc="S")
        e_mml = P.op("T", lambda t: t.matmul(PA[:, 0:16], ones1[0:1, :], lnl_f[0:1, 0:16],
                                             start=True, stop=True), [e_lnl, e_gp], inc="T")
        e_lnlbc = P.op("V", lambda v: v.tensor_copy(lnlbc[:], PA[:, 0:16]), [e_mml], inc="V")
        P.op("T", lambda t: t.matmul(PB[:, 0:512], ones1[0:1, :], carr_f[0:1, 0:512],
                                     start=True, stop=True), [e_gam, e_gp])
        e_mg1 = P.op("T", lambda t: t.matmul(PB[:, 512:1024], ones1[0:1, :],
                                             carr_f[0:1, 512:1024], start=True, stop=True),
                     inc="T")
        e_gbc = P.op("S", lambda s: s.copy(gam_bc[:], PB[:]), [e_mg1], inc="S")
        e_bet = ydma(lambda s: s.dma_start(out=carr_f[0:1, :], in_=bet_d[None, :]),
                     [e_gbc], ch="carr")
        P.op("T", lambda t: t.matmul(PC[:, 0:512], ones1[0:1, :], carr_f[0:1, 0:512],
                                     start=True, stop=True), [e_bet])
        e_mb1 = P.op("T", lambda t: t.matmul(PC[:, 512:1024], ones1[0:1, :],
                                             carr_f[0:1, 512:1024], start=True, stop=True),
                     inc="T")
        e_bbc = P.op("S", lambda s: s.copy(bet_bc[:], PC[:]), [e_mb1], inc="S")
        e_cz = P.op("V", lambda v: v.engine_nop(), [e_mb1], inc="V")

        Tt = [Ttmp0, Ttmp1]
        ev_texp = {}
        for h in range(H):
            tp = Tt[h % 2]
            w = [e_lnlbc, e_id]
            if h >= 1:
                w.append(ev_texp[h - 1])
            ev_tmul = P.op("V", lambda v, tp=tp, h=h: v.tensor_scalar(
                tp[:], E_f[:], lnlbc[:, h : h + 1], None, ALU.mult), w, inc="V")
            ev_texp[h] = P.op("S", lambda s, tp=tp, h=h: s.activation(
                tmat[:, h * 128 : (h + 1) * 128], tp[:], AF.Exp), [ev_tmul], inc="S")
        last_texp = ev_texp[H - 1]
        e_mmc = P.op("T", lambda t: t.matmul(PD[0:16, 0:1], lnl_f[0:1, 0:16],
                                             ones1[0:1, 0:1], start=True, stop=True),
                     [e_lnl, e_gp], inc="T")
        e_lnc = P.op("V", lambda v: v.tensor_copy(lnl_c[:], PD[0:16, 0:1]),
                     [e_mmc], inc="V")
        e_av1 = P.op("V", lambda v: v.tensor_scalar(av16f[:], av16f[:], lnl_c[:],
                                                    None, ALU.mult), [e_lnc], inc="V")
        e_av = P.op("S", lambda s: s.activation(av16f[:], av16f[:], AF.Exp),
                    [e_av1], inc="S")
        e_cm1 = P.op("V", lambda v: v.tensor_tensor(cmatf[:], cmatf[:], lnlbc[:, 0:16],
                                                    ALU.mult), [e_lnlbc, e_gp], inc="V")
        e_cm = P.op("S", lambda s: s.activation(cmat[:], cmatf[:], AF.Exp),
                    [e_cm1], inc="S")
        e_l1 = P.op("V", lambda v: v.tensor_scalar(lnl_c[:], lnl_c[:], 128.0, None,
                                                   ALU.mult), [e_av1], inc="V")
        e_l2 = P.op("S", lambda s: s.activation(lnl_c[:], lnl_c[:], AF.Exp),
                    [e_l1], inc="S")
        e_l128 = P.op("V", lambda v: v.tensor_scalar(l128d[:], l128d[:], lnl_c[:],
                                                     None, ALU.mult), [e_l2], inc="V")

        wcast = {}
        wsrc = [(wqT_d, wq_bf), (wvT_d, wv_bf), (wgT_d, wg_bf), (woT_d, wo_bf)]
        NFS = min(CPG, 8)
        for wi, (wd, wb) in enumerate(wsrc):
            for k in range(KC):
                i = wi * KC + k
                slot = i % NFS
                w = [wcast[i - NFS]] if i >= NFS else []
                wl = ydma(lambda s, wd=wd, k=k, slot=slot: s.dma_start(
                    out=facc[:, slot, :], in_=wd[k * 128 : (k + 1) * 128, :]),
                    w, ch="fac%d" % slot)
                wcast[i] = P.op("S", lambda s, wb=wb, k=k, slot=slot: s.copy(
                    wb[:, k * D : (k + 1) * D], facc[:, slot, :]), [wl], inc="S")
        e_wdone = wcast[4 * KC - 1]

        E_xt, E_xf, E_xc = {}, {}, {}
        E_tq, E_tv, E_tg, E_tsc, E_ty2, E_to, E_tx2 = {}, {}, {}, {}, {}, {}, {}
        E_qe, E_ve, E_gs, E_yl, E_y2, E_y2e = {}, {}, {}, {}, {}, {}
        E_r, E_x2, E_x2b, E_x2e, E_st = {}, {}, {}, {}, {}
        E_tce = {}
        E_crf, E_crb = {}, {}
        E_nr1, E_nr2, E_sq2, E_se, E_y0, E_hbd, E_gsd, E_s2 = {}, {}, {}, {}, {}, {}, {}, {}
        scr_prev = [None]
        E_fl, E_fc = {}, {}
        E_gate, E_up, E_hT, E_dn, E_sil, E_hb, E_hf, E_fa = {}, {}, {}, {}, {}, {}, {}, {}
        E_vffn = {}

        def slots(par):
            b = par * 24
            return dict(sy1=b + 0, sy2=b + 1, nm1=b + 2, tt1=b + 3, ve1=b + 4,
                        rs1=b + 5, tn1=b + 6, nmr=b + 8, sr1=b + 9, sr2=b + 10,
                        nm2=b + 11, tt2=b + 12, ve2=b + 13, rs2=b + 14, tn2=b + 15)

        def ln_stats(s1, s2, nm, tt_, ve, waits):
            e1 = vchain(lambda v: v.tensor_scalar(sl(nm), sl(s1), -1.0 / D, None, ALU.mult),
                        waits)
            vchain(lambda v: v.tensor_scalar(sl(ve), sl(s2), 1.0 / D, LN_EPS,
                                             ALU.mult, ALU.add))
            e2 = vchain(lambda v: v.tensor_tensor(sl(tt_), sl(nm), sl(nm), ALU.mult), [e1])
            return vchain(lambda v: v.tensor_tensor(sl(ve), sl(ve), sl(tt_), ALU.subtract),
                          [e2])

        def x_load(gc):
            par = gc % 2
            w = [E_xt.get(gc - 1)] if gc >= 1 else [e_wdone]
            t0 = gc * 128
            cf1 = ydma(lambda s, t0=t0: s.dma_start(
                out=stage[:].rearrange("p (k t) -> p k t", k=KC),
                in_=xT_d.rearrange("(k p) t -> p k t", p=128)[:, :, t0 : t0 + 128]), w)
            E_xc[gc] = P.op("S", lambda s, par=par: s.copy(xf_bf[par][:], stage[:]),
                            [cf1, E_tg.get(gc - 2)], inc="S")
            cf2 = ydma(lambda s, t0=t0: s.dma_start(
                out=stage[:], in_=x_d[t0 : t0 + 128, :]), [E_xc[gc]])
            E_xt[gc] = P.op("S", lambda s, par=par: s.copy(xtok[par][:], stage[:]),
                            [cf2, E_r.get(gc - 2)], inc="S")

        def s_xfcast(gc):
            pass

        def t_qvg(gc):
            par = gc % 2
            g = gc // CPG
            first_w = [E_xc[gc], E_r.get(gc - 2), E_qe.get(gc - 1)]
            if gc == 0:
                first_w += [e_lnlbc, e_gbc, e_bbc]
            if gc % CPG == 0 and g >= 1:
                first_w.append(E_vffn.get(g - 1))
            for (dst, wb, tab, extra) in (
                (PA, wq_bf, E_tq, []),
                (PC, wg_bf, E_tg, [E_gs.get(gc - 1)]),
                (PB, wv_bf, E_tv, [E_crf.get(gc - 1), E_x2e.get(gc - 3)]),
            ):
                w = first_w + extra
                first_w = []
                ev = None
                for half in range(2):
                    for k in range(KC):
                        ev = P.op("T", lambda t, dst=dst, wb=wb, par=par, k=k, half=half:
                                  t.matmul(
                                      dst[:, half * 512 : (half + 1) * 512],
                                      xf_bf[par][:, k * 128 : (k + 1) * 128],
                                      wb[:, k * D + half * 512 : k * D + (half + 1) * 512],
                                      start=(k == 0), stop=(k == KC - 1)),
                                  w if (half == 0 and k == 0) else (),
                                  inc="T" if (half == 1 and k == KC - 1) else None)
                tab[gc] = ev

        def s_qv_evac(gc):
            par = gc % 2
            E_qe[gc] = P.op("S", lambda s, par=par: s.copy(qbf[par][:], PA[:]),
                            [E_tq[gc], E_s2.get(gc - 2)], inc="S")
            E_ve[gc] = P.op("S", lambda s, par=par: s.copy(vbf[par][:], PB[:]),
                            [E_tv[gc], E_y0.get(gc - 2)], inc="S")

        def s_silu(gc):
            par = gc % 2
            E_gs[gc] = P.op("S", lambda s, par=par: s.activation(gsb[par][:], PC[:], AF.Silu),
                            [E_tg[gc]], inc="S")

        def t_scan(gc):
            par = gc % 2
            w = [E_ve[gc], e_av, last_texp, e_cz, e_l128, e_cm,
                 E_crf.get(gc - 1), E_y2e.get(gc - 1)]
            P.op("T", lambda t: t.matmul(PD[:, 0:512], av16f[:], cr16f[:, 0:512],
                                         start=True, stop=True), w)
            P.op("T", lambda t: t.matmul(PD[:, 512:1024], av16f[:], cr16f[:, 512:1024],
                                         start=True, stop=True))
            ev = None
            for h in range(H):
                ev = P.op("T", lambda t, par=par, h=h: t.matmul(
                    PD[:, h * HD : (h + 1) * HD], tmat[:, h * 128 : (h + 1) * 128],
                    vbf[par][:, h * HD : (h + 1) * HD], start=False, stop=True,
                    skip_group_check=True),
                    inc="T" if h == H - 1 else None)
            E_tsc[gc] = ev
            wce = [E_x2e.get(gc - 2), E_crf.get(gc - 1)]
            P.op("T", lambda t: t.matmul(PB[0:16, 0:512], l128d[:], cr16f[:, 0:512],
                                         start=True, stop=False), wce)
            P.op("T", lambda t: t.matmul(PB[0:16, 512:1024], l128d[:],
                                         cr16f[:, 512:1024], start=True, stop=False))
            P.op("T", lambda t, par=par: t.matmul(PB[0:16, 0:512], cmat[:],
                                                  vbf[par][:, 0:512],
                                                  start=False, stop=True))
            E_tce[gc] = P.op("T", lambda t, par=par: t.matmul(
                PB[0:16, 512:1024], cmat[:], vbf[par][:, 512:1024],
                start=False, stop=True), inc="T")

        def v_carr16(gc):
            P.op("V", lambda v: v.tensor_tensor(
                cr16f[:], mask16[:], PB[0:16, :], ALU.mult), [E_tce[gc]])
            P.op("V", lambda v: v.engine_nop())
            E_crf[gc] = P.op("V", lambda v: v.engine_nop(), inc="V")

        def v_y_ln1(gc):
            par = gc % 2
            sm = slots(par)
            E_se[gc] = P.op("S", lambda s, par=par: s.copy(vbf[par][:], PD[:]),
                            [E_tce[gc]], inc="S")
            wy = [E_se[gc], E_qe[gc]]
            if dbg == "ffn2":
                evs = P.op("S", lambda s: s.copy(scr_bf[:], PD[:]), [E_tsc[gc]], inc="S")
                wy.append(P.op("G", lambda gp, gc=gc: gp.dma_start(
                    out=dst_d[gc * 128 : (gc + 1) * 128, :], in_=scr_bf[:]),
                    [evs], inc=("G", 16)))
            e_y0 = P.op("V", lambda v, par=par, sm=sm: v.scalar_tensor_tensor(
                ybf[par][:], qbf[par][:], 1.0, vbf[par][:], ALU.mult, ALU.mult,
                accum_out=sl(sm["sy1"])), wy, inc="V")
            E_y0[gc] = e_y0
            w_s2 = [e_y0]
            if dbg == "ffn2":
                t0p = gc * 128
                evs2 = P.op("S", lambda s: s.copy(scr_bf[:], PD[:]), [e_y0], inc="S")
                P.op("G", lambda gp, t0p=t0p: gp.dma_start(
                    out=dst2_d[t0p : t0p + 128, :], in_=scr_bf[:]),
                    [evs2], inc=("G", 16))
                w_s2.append(P.op("G", lambda gp, par=par, t0p=t0p: gp.dma_start(
                    out=dbg_d["dy"][t0p : t0p + 128, :], in_=ybf[par][:]),
                    [e_y0], inc=("G", 16)))
            e_s2 = P.op("V", lambda v, par=par, sm=sm: v.scalar_tensor_tensor(
                qbf[par][:], ybf[par][:], 1.0, ybf[par][:], ALU.mult, ALU.mult,
                accum_out=sl(sm["sy2"])), w_s2, inc="V")
            E_s2[gc] = e_s2
            e = ln_stats(sm["sy1"], sm["sy2"], sm["nm1"], sm["tt1"], sm["ve1"], [e_s2])
            e = nr_rsqrt(sm["ve1"], sm["rs1"], sm["tn1"], e)
            E_nr1[gc] = P.op("V", lambda v, sm=sm: v.tensor_tensor(
                sl(sm["nmr"]), sl(sm["nm1"]), sl(sm["rs1"]), ALU.mult), [e], inc="V")

        def s_yln(gc):
            par = gc % 2
            sm = slots(par)
            E_yl[gc] = P.op("S", lambda s, par=par, sm=sm: s.activation(
                yy[par][:], ybf[par][:], AF.Identity, bias=sl(sm["nmr"]), scale=sl(sm["rs1"])),
                [E_nr1[gc]], inc="S")

        def v_y2(gc):
            par = gc % 2
            e1 = P.op("V", lambda v, par=par: v.tensor_tensor(
                yy[par][:], yy[par][:], gam_bc[:], ALU.mult), [E_yl[gc]], inc="V")
            e2 = P.op("V", lambda v, par=par: v.tensor_tensor(
                yy[par][:], yy[par][:], bet_bc[:], ALU.add), [e1], inc="V")
            E_y2[gc] = P.op("V", lambda v, par=par: v.tensor_tensor(
                yy[par][:], yy[par][:], gsb[par][:], ALU.mult), [E_gs[gc], e2], inc="V")

        def t_y2T(gc):
            par = gc % 2
            w = [E_y2[gc], E_se[gc]]
            if dbg == "ffn2":
                t0p = gc * 128
                w.append(P.op("G", lambda gp, par=par, t0p=t0p: gp.dma_start(
                    out=dbg_d["dyy"][t0p : t0p + 128, :], in_=yy[par][:]),
                    [E_y2[gc]], inc=("G", 16)))
            ev = None
            for k in range(KC):
                ev = P.op("T", lambda t, par=par, k=k: t.matmul(
                    PD[:, k * 128 : (k + 1) * 128], yy[par][:, k * 128 : (k + 1) * 128],
                    ident[:], start=True, stop=True), w if k == 0 else (),
                    inc="T" if k == KC - 1 else None)
            E_ty2[gc] = ev

        def s_y2f(gc):
            par = gc % 2
            E_y2e[gc] = P.op("S", lambda s, par=par: s.copy(y2f[par][:], PD[:]),
                             [E_ty2[gc]], inc="S")

        def t_o(gc):
            par = gc % 2
            w = [E_y2e[gc], E_qe.get(gc + 1), E_r.get(gc - 1)]
            ev = None
            for half in range(2):
                for k in range(KC):
                    ev = P.op("T", lambda t, par=par, k=k, half=half: t.matmul(
                        PA[:, half * 512 : (half + 1) * 512],
                        y2f[par][:, k * 128 : (k + 1) * 128],
                        wo_bf[:, k * D + half * 512 : k * D + (half + 1) * 512],
                        start=(k == 0), stop=(k == KC - 1)),
                        w if (half == 0 and k == 0) else (),
                        inc="T" if (half == 1 and k == KC - 1) else None)
            E_to[gc] = ev

        def v_ln2(gc):
            par = gc % 2
            cg = gc % CPG
            g = gc // CPG
            sm = slots(par)
            w = [E_to[gc], E_xt[gc]]
            if g >= 1:
                w.append(("O", 16 * g * CPG))
            E_r[gc] = P.op("V", lambda v, par=par, cg=cg, sm=sm: v.scalar_tensor_tensor(
                facc[:, cg, :], xtok[par][:], 1.0, PA[:], ALU.mult, ALU.add,
                accum_out=sl(sm["sr1"])), w, inc="V")

        def s_sq2(gc):
            cg = gc % CPG
            sm = slots(gc % 2)
            E_sq2[gc] = P.op("S", lambda s, cg=cg, sm=sm: s.activation(
                scr_bf[:], facc[:, cg, :], AF.Square, accum_out=sl(sm["sr2"])),
                [E_r[gc], scr_prev[0]], inc="S")
            scr_prev[0] = E_sq2[gc]

        def v_x2(gc):
            cg = gc % CPG
            sm = slots(gc % 2)
            w2 = [E_sq2[gc]]
            if dbg in ("probe", "ffn2"):
                cg_ = gc % CPG
                t0_ = gc * 128
                ev = P.op("G", lambda gp, cg_=cg_, t0_=t0_: gp.dma_start(
                    out=dbg_d["dro"][t0_ : t0_ + 128, :], in_=facc[:, cg_, :]),
                    [E_sq2[gc]], inc=("G", 16))
                w2.append(ev)
            e = ln_stats(sm["sr1"], sm["sr2"], sm["nm2"], sm["tt2"], sm["ve2"], w2)
            e = nr_rsqrt(sm["ve2"], sm["rs2"], sm["tn2"], e)
            E_x2[gc] = P.op("V", lambda v, cg=cg, sm=sm: v.tensor_scalar(
                facc[:, cg, :], facc[:, cg, :], sl(sm["nm2"]), sl(sm["rs2"]),
                ALU.add, ALU.mult), [e], inc="V")

        def s_x2b(gc):
            par = gc % 2
            cg = gc % CPG
            E_x2b[gc] = P.op("S", lambda s, par=par, cg=cg: s.copy(
                x2bf[par][:], facc[:, cg, :]), [E_x2[gc]], inc="S")

        def t_x2F(gc):
            par = gc % 2
            w = [E_x2b[gc], E_ve.get(gc + 2), E_x2e.get(gc - 1), E_crf.get(gc),
                 E_crf.get(gc + 1)]
            ev = None
            for k in range(KC):
                ev = P.op("T", lambda t, par=par, k=k: t.matmul(
                    PB[:, k * 128 : (k + 1) * 128], x2bf[par][:, k * 128 : (k + 1) * 128],
                    ident[:], start=True, stop=True), w if k == 0 else (),
                    inc="T" if k == KC - 1 else None)
            E_tx2[gc] = ev

        def s_x2e(gc):
            cg = gc % CPG
            E_x2e[gc] = P.op("S", lambda s, cg=cg: s.copy(
                x2f[:, :, cg * 128 : (cg + 1) * 128],
                PB[:].rearrange("p (k t) -> p k t", k=KC)), [E_tx2[gc]], inc="S")

        wgq_r = wgq_d.rearrange("(k p) h -> p k h", p=128)
        wuq_r = wuq_d.rearrange("(k p) h -> p k h", p=128)
        wpump = []

        conv_pieces = []
        for cc in range(DH // 1024):
            for k in range(KC):
                for hh in range(2):
                    conv_pieces.append(("g", wgq_d, wgb_d, k * 128, cc * 1024 + hh * 512))
            for k in range(KC):
                for hh in range(2):
                    conv_pieces.append(("u", wuq_d, wub_d, k * 128, cc * 1024 + hh * 512))
            for p in range(8):
                for hh in range(2):
                    conv_pieces.append(("d", wdq_d, wdb_d, cc * 1024 + p * 128, hh * 512))
        NCONV = len(conv_pieces)
        conv_state = {"n": 0}
        E_cvc = {}

        def conv_pump(nsteps):
            for _ in range(nsteps):
                p = conv_state["n"]
                if p >= NCONV:
                    return
                conv_state["n"] = p + 1
                kind, srcd, dstd, r0, c0 = conv_pieces[p]
                par = p % 2
                cf = ydma(lambda s, srcd=srcd, r0=r0, c0=c0, par=par: s.dma_start(
                    out=cvt32[par][:], in_=srcd[r0 : r0 + 128, c0 : c0 + 512]),
                    [E_cvc.get(p - 2)], ch="cvt%d" % par)
                E_cvc[p] = P.op("G", lambda g, par=par: g.tensor_copy(
                    cvtbf[par][:], cvt32[par][:]),
                    [cf, ("Dcvs%d" % par, 16 * (p // 2))], inc="G")
                P.op("G", lambda g, dstd=dstd, r0=r0, c0=c0, par=par: g.dma_start(
                    out=dstd[r0 : r0 + 128, c0 : c0 + 512], in_=cvtbf[par][:]),
                    [E_cvc[p]], inc=("Dcvs%d" % par, 16))

        def conv_block_done(s_):
            blk = min(s_ * SLICE // 1024, DH // 1024 - 1)
            return [("Dcvs0", 16 * 24 * (blk + 1)), ("Dcvs1", 16 * 24 * (blk + 1))]

        wgb_r = wgb_d.rearrange("(k p) h -> p k h", p=128)
        wub_r = wub_d.rearrange("(k p) h -> p k h", p=128)

        def ffn_wload(gsl):
            par = gsl % 2
            g_ = gsl // NS
            s_ = gsl % NS
            s0 = s_ * SLICE
            lastit = lambda g2: (g2 + 1) * (GROUP // 512) - 1

            def st_g(x=None):
                w = [E_gate.get((lastit(gsl - 2), JS - 1))]
                if g_ == 0:
                    w.extend(conv_block_done(s_))
                E_fc[("gc", gsl)] = ydma(lambda s: s.dma_start(
                    out=wgs[par][:].rearrange("p (k h) -> p k h", k=KC),
                    in_=wgb_r[:, :, s0 : s0 + SLICE]), w, ch="wgs%d" % par)

            def st_u(x=None):
                w = [E_up.get((lastit(gsl - 2), JS - 1))]
                E_fc[("uc", gsl)] = ydma(lambda s: s.dma_start(
                    out=wus[par][:].rearrange("p (k h) -> p k h", k=KC),
                    in_=wub_r[:, :, s0 : s0 + SLICE]), w, ch="wus%d" % par)

            def st_d(x=None):
                w = [E_dn.get(lastit(gsl - 2))]
                E_fc[("dc", gsl)] = ydma(lambda s: s.dma_start(
                    out=wds[par][:].rearrange("p (j d) -> p j d", j=JS),
                    in_=wdb_d[s0 : s0 + SLICE, :].rearrange("(j p) d -> p j d", p=128)),
                    w, ch="wds%d" % par)

            wpump.extend([st_g, st_u, st_d])

        PUMP_N = max(-(-3 // max(GROUP // 512, 1)), -(-3 // CPG), 1) + 1

        def pump():
            for _ in range(PUMP_N):
                if wpump:
                    wpump.pop(0)()

        NB = GROUP // 512
        TB = 512

        def ffn_gu(fi):
            sb = fi // NB
            g = sb // NS
            b = fi % NB
            par = sb % 2
            t0 = b * TB
            for j in range(JS):
                if j == 0:
                    w = [E_fc[("gc", sb)], E_sil.get((fi - 1, 0)), E_x2e[g * CPG + CPG - 1]]
                else:
                    w = [E_sil.get((fi - 1, 1))]
                for k in range(KC):
                    ev = P.op("T", lambda t, j=j, k=k, par=par, t0=t0: t.matmul(
                        PA[:, j * 512 : (j + 1) * 512],
                        wgs[par][:, k * SLICE + j * 128 : k * SLICE + (j + 1) * 128],
                        x2f[:, k, t0 : t0 + TB],
                        start=(k == 0), stop=(k == KC - 1)),
                        w if k == 0 else (),
                        inc="T" if k == KC - 1 else None)
                E_gate[(fi, j)] = ev
                w = [E_fc[("uc", sb)], E_hb.get((fi - 1, j))]
                for k in range(KC):
                    ev = P.op("T", lambda t, j=j, k=k, par=par, t0=t0: t.matmul(
                        PB[:, j * 512 : (j + 1) * 512],
                        wus[par][:, k * SLICE + j * 128 : k * SLICE + (j + 1) * 128],
                        x2f[:, k, t0 : t0 + TB],
                        start=(k == 0), stop=(k == KC - 1)),
                        w if k == 0 else (),
                        inc="T" if k == KC - 1 else None)
                E_up[(fi, j)] = ev

        def ffn_sil(fi):
            ip = fi % 2
            for j in range(JS):
                E_sil[(fi, j)] = P.op("S", lambda s, ip=ip, j=j: s.activation(
                    gs2[ip][:, j * 512 : (j + 1) * 512],
                    PA[:, j * 512 : (j + 1) * 512], AF.Silu, scale=QS),
                    [E_gate[(fi, j)], E_hb.get((fi - 1, j))], inc="S")
            for j in range(JS):
                E_hb[(fi, j)] = P.op("V", lambda v, ip=ip, j=j: v.scalar_tensor_tensor(
                    hbf[ip][:, j * 512 : (j + 1) * 512],
                    gs2[ip][:, j * 512 : (j + 1) * 512], QS,
                    PB[:, j * 512 : (j + 1) * 512], ALU.mult, ALU.mult),
                    [E_sil[(fi, j)], E_up[(fi, j)], E_dn.get(fi - 2)], inc="V")

        def ffn_down(g, s_, b, fi):
            sb = g * NS + s_
            par = sb % 2
            ip = fi % 2
            base = g * CPG
            last = s_ == NS - 1
            e_last = {}
            for tc in range(4):
                cg = b * 4 + tc
                bank = PC if tc % 2 == 0 else PD
                w = [E_hb[(fi, 0)], E_hb[(fi, 1)],
                     E_fa.get((fi - 1, tc - 2 + 4 * ((tc - 2) < 0))),
                     E_fc[("dc", sb)]]
                if tc >= 2:
                    w = [E_fa.get((fi, tc - 2))]
                ev = None
                for j in range(JS):
                    for half in range(2):
                        ev = P.op("T", lambda t, j=j, half=half, par=par, ip=ip,
                                  tc=tc, bank=bank: t.matmul(
                            bank[:, half * 512 : (half + 1) * 512],
                            hbf[ip][:, j * 512 + tc * 128 : j * 512 + (tc + 1) * 128],
                            wds[par][:, j * D + half * 512 : j * D + (half + 1) * 512],
                            start=(j == 0), stop=(j == JS - 1)),
                            w if (j == 0 and half == 0) else (),
                            inc="T" if (j == JS - 1 and half == 1) else None)
                E_dntc = ev
                acc = sl(48 + cg) if last else None
                E_fa[(fi, tc)] = P.op("V", lambda v, cg=cg, acc=acc, bank=bank:
                                      v.scalar_tensor_tensor(
                    facc[:, cg, :], bank[:], QS, facc[:, cg, :], ALU.mult, ALU.add,
                    accum_out=acc), [E_dntc], inc="V")
                e_last[cg] = E_fa[(fi, tc)]
            E_dn[fi] = ev
            return e_last

        def ln3_out(gc, e_lastadd):
            cg = gc % CPG
            par = gc % 2
            s1 = 48 + cg
            s2 = 56 + cg
            nm = 64 + par * 8
            tt_ = 65 + par * 8
            ve = 66 + par * 8
            rs = 67 + par * 8
            tn = 68 + par * 8
            E_s3 = P.op("S", lambda s, cg=cg, s2=s2: s.activation(
                scr_bf[:], facc[:, cg, :], AF.Square, accum_out=sl(s2)),
                [e_lastadd, scr_prev[0]], inc="S")
            scr_prev[0] = E_s3
            wl3 = [E_s3]
            if dbg == "ffn2":
                t0p = gc * 128
                g_p = gc // CPG
                evd = P.op("G", lambda gp, cg=cg, t0p=t0p: gp.dma_start(
                    out=dr2_d[t0p : t0p + 128, :], in_=facc[:, cg, :]),
                    [e_lastadd], inc=("G", 16))
                wl3.append(evd)
                if cg == CPG - 1:
                    P.op("G", lambda gp, g_p=g_p: gp.dma_start(
                        out=dxf_d[g_p * 128 : (g_p + 1) * 128, :],
                        in_=x2f[:].rearrange("p a b -> p (a b)")), inc=("G", 16))
            e = ln_stats(s1, s2, nm, tt_, ve, wl3)
            e = nr_rsqrt(ve, rs, tn, e)
            e_x3 = P.op("V", lambda v, cg=cg, nm=nm, rs=rs: v.tensor_scalar(
                facc[:, cg, :], facc[:, cg, :], sl(nm), sl(rs), ALU.add, ALU.mult),
                [e], inc="V")
            t0 = gc * 128
            E_st[gc] = P.op("G", lambda gp, cg=cg, t0=t0: gp.dma_start(
                out=y_d[t0 : t0 + 128, :], in_=facc[:, cg, :]), [e_x3], inc=("O", 16))

        wload_done = set()

        def wload_once(gsl):
            if gsl < NG * NS and gsl not in wload_done:
                wload_done.add(gsl)
                ffn_wload(gsl)

        CPUMP = -(-NCONV // CPG)
        for g in range(NG):
            base = g * CPG
            x_load(base)
            s_xfcast(base)
            x_load(base + 1)
            if g >= 1:
                wload_once(g * NS + 0)
                wload_once(g * NS + 1)
            t_qvg(base)
            s_qv_evac(base)
            s_silu(base)
            for cg in range(CPG):
                gc = base + cg
                nxt = gc + 1
                t_scan(gc)
                v_carr16(gc)
                if dbg == "probe":
                    P.op("G", lambda gp, gc=gc: gp.dma_start(
                        out=dcr_d[gc * 16 : (gc + 1) * 16, :], in_=cr16f[:]),
                        [E_crf[gc]], inc=("O", 16))
                if g == 0:
                    conv_pump(CPUMP)
                pump()
                v_y_ln1(gc)
                s_yln(gc)
                v_y2(gc)
                if cg + 1 < CPG:
                    s_xfcast(nxt)
                    t_qvg(nxt)
                    s_qv_evac(nxt)
                    s_silu(nxt)
                if cg >= 1:
                    t_x2F(gc - 1)
                    s_x2e(gc - 1)
                t_y2T(gc)
                s_y2f(gc)
                t_o(gc)
                v_ln2(gc)
                if cg + 2 < CPG:
                    x_load(base + cg + 2)
                s_sq2(gc)
                pump()
                v_x2(gc)
                s_x2b(gc)
                if dbg in ("probe", "ffn2"):
                    par = gc % 2
                    t0 = gc * 128
                    for nm, src_ap, ev in (("dq", qbf[par][:], E_qe[gc]),
                                           ("dv", vbf[par][:], E_ve[gc]),
                                           ("dgs", gsb[par][:], E_gs[gc]),
                                           ("dy2f", y2f[par][:], E_y2e[gc]),
                                           ("dx2", facc[:, cg, :], E_x2[gc])):
                        P.op("G", lambda gp, nm=nm, src_ap=src_ap, t0=t0: gp.dma_start(
                            out=dbg_d[nm][t0 : t0 + 128, :], in_=src_ap),
                            [ev], inc=("O", 16))
            t_x2F(base + CPG - 1)
            s_x2e(base + CPG - 1)
            if g == 0:
                conv_pump(NCONV)
                wload_once(0)
                wload_once(1)
                for _ in range(6):
                    pump()
            if dbg == "probe":
                P.op("G", lambda gp: gp.dma_start(out=dtm_d, in_=tmat[:]),
                     [last_texp], inc=("O", 16))

                P.op("G", lambda gp: gp.dma_start(out=dsum_d, in_=sums[:]),
                     [E_x2[base + CPG - 1]], inc=("O", 16))
                for cg in range(CPG):
                    t0 = (base + cg) * 128
                    P.op("G", lambda gp, cg=cg, t0=t0: gp.dma_start(
                        out=y_d[t0 : t0 + 128, :], in_=facc[:, cg, :]),
                        [E_x2[base + cg]], inc=("O", 16))
                continue
            if dbg == "x2":
                for cg in range(CPG):
                    t0 = (base + cg) * 128
                    P.op("G", lambda gp, cg=cg, t0=t0: gp.dma_start(
                        out=y_d[t0 : t0 + 128, :], in_=facc[:, cg, :]),
                        [E_x2[base + cg]], inc=("O", 16))
                continue
            fi0 = (g * NS) * NB
            last_fi = (g * NS + NS) * NB - 1
            ffn_gu(fi0)
            for s_ in range(NS):
                wload_once(g * NS + s_ + 1)
                for b in range(NB):
                    fi = (g * NS + s_) * NB + b
                    pump()
                    ffn_sil(fi)
                    if fi + 1 <= last_fi:
                        ffn_gu(fi + 1)
                    e_last = ffn_down(g, s_, b, fi)
                    if s_ == NS - 1:
                        for cg2, ev2 in e_last.items():
                            ln3_out(base + cg2, ev2)
            E_vffn[g] = P.op("V", lambda v: v.engine_nop(), inc="V")

        ydma_flush()
        total_o = P.cnt.get("O", 0)
        P.op("G", lambda gp: gp.nop(), [("O", total_o)])

        def runner(names):
            def body(eng):
                for (e, fn, ws, inc) in P.ops:
                    if e not in names:
                        continue
                    for (s, v) in ws:
                        eng.wait_ge(SEM[s], v)
                    inst = fn(eng)
                    if inc is not None and inst is not None:
                        inst.then_inc(SEM[inc[0]], inc[1])
            return body

        block.tensor(runner(("T",)))
        block.vector(runner(("V",)))
        block.sync(runner(("Y",)))
        block.gpsimd(runner(("G",)))
        block.scalar(runner(("S", "Q")))

    return nc


_cache = {}


def _get_nc(L, GROUP, SLICE):
    key = (L, GROUP, SLICE)
    if key not in _cache:
        _cache[key] = _build(L, GROUP, SLICE)
    return _cache[key]


def kernel(L0_attn_x, L0_attn_decay, L0_attn_Wq, L0_attn_Wv, L0_attn_Wg, L0_attn_Wo,
           L0_attn_ln_gamma, L0_attn_ln_beta, L0_ffn_w_gate, L0_ffn_w_up, L0_ffn_w_down):
    x = np.ascontiguousarray(np.asarray(L0_attn_x, dtype=np.float32))
    B, L, _ = x.shape
    nc = _get_nc(L, 1024, 256)
    shared = {
        "wqT": np.ascontiguousarray(np.asarray(L0_attn_Wq, np.float32).T),
        "wvT": np.ascontiguousarray(np.asarray(L0_attn_Wv, np.float32).T),
        "wgT": np.ascontiguousarray(np.asarray(L0_attn_Wg, np.float32).T),
        "woT": np.ascontiguousarray(np.asarray(L0_attn_Wo, np.float32).T),
        "dec": np.ascontiguousarray(np.asarray(L0_attn_decay, np.float32)),
        "gam": np.ascontiguousarray(np.asarray(L0_attn_ln_gamma, np.float32)),
        "bet": np.ascontiguousarray(np.asarray(L0_attn_ln_beta, np.float32)),
        "wgq": np.ascontiguousarray(np.asarray(L0_ffn_w_gate, np.int32).T),
        "wuq": np.ascontiguousarray(np.asarray(L0_ffn_w_up, np.int32).T),
        "wdq": np.ascontiguousarray(np.asarray(L0_ffn_w_down, np.int32).T),
    }
    in_maps = []
    for b in range(B):
        m = dict(shared)
        m["x"] = x[b]
        m["xT"] = np.ascontiguousarray(x[b].T)
        in_maps.append(m)
    res = run_bass_kernel_spmd(nc, in_maps, core_ids=list(range(B)))
    return np.stack([np.asarray(r["y"], dtype=np.float32) for r in res.results])


# version 15
# speedup vs baseline: 1.0177x; 1.0177x over previous
import numpy as np
import ml_dtypes
import concourse.bass as bass
import concourse.mybir as mybir
from concourse.bass_utils import run_bass_kernel_spmd

F32 = mybir.dt.float32
BF16 = mybir.dt.bfloat16
I32 = mybir.dt.int32
AF = mybir.ActivationFunctionType
ALU = mybir.AluOpType

D = 1024
H = 16
HD = 64
DH = 3072
KC = D // 128
LN_EPS = 1e-5
QS = 1.0 / 127.0
MAGIC = 0x5F3759DF


class Plan:
    def __init__(self):
        self.ops = []
        self.cnt = {}

    def op(self, eng, fn, waits=(), inc=None):
        rec = None
        if inc is not None:
            sem, n = inc if isinstance(inc, tuple) else (inc, 1)
            self.cnt[sem] = self.cnt.get(sem, 0) + n
            rec = (sem, n)
        ws = [w for w in waits if w is not None and w[1] > 0]
        self.ops.append((eng, fn, ws, rec))
        return (rec[0], self.cnt[rec[0]]) if rec else None


def _build(L, GROUP, SLICE, dbg=None):
    CH = L // 128
    NG = L // GROUP
    CPG = GROUP // 128
    NS = DH // SLICE
    JS = SLICE // 128

    nc = bass.Bass("TRN2", target_bir_lowering=False)
    x_d = nc.dram_tensor("x", [L, D], F32, kind="ExternalInput").ap()
    xT_d = nc.dram_tensor("xT", [D, L], F32, kind="ExternalInput").ap()
    wqT_d = nc.dram_tensor("wqT", [D, D], F32, kind="ExternalInput").ap()
    wvT_d = nc.dram_tensor("wvT", [D, D], F32, kind="ExternalInput").ap()
    wgT_d = nc.dram_tensor("wgT", [D, D], F32, kind="ExternalInput").ap()
    woT_d = nc.dram_tensor("woT", [D, D], F32, kind="ExternalInput").ap()
    dec_d = nc.dram_tensor("dec", [H], F32, kind="ExternalInput").ap()
    gam_d = nc.dram_tensor("gam", [D], F32, kind="ExternalInput").ap()
    bet_d = nc.dram_tensor("bet", [D], F32, kind="ExternalInput").ap()
    wgq_d = nc.dram_tensor("wgq", [D, DH], I32, kind="ExternalInput").ap()
    wuq_d = nc.dram_tensor("wuq", [D, DH], I32, kind="ExternalInput").ap()
    wdq_d = nc.dram_tensor("wdq", [DH, D], I32, kind="ExternalInput").ap()
    y_d = nc.dram_tensor("y", [L, D], F32, kind="ExternalOutput").ap()
    wgb_d = nc.dram_tensor("wgb", [D, DH], BF16, kind="Internal").ap()
    wub_d = nc.dram_tensor("wub", [D, DH], BF16, kind="Internal").ap()
    wdb_d = nc.dram_tensor("wdb", [DH, D], BF16, kind="Internal").ap()
    if dbg in ("probe", "ffn2"):
        dbg_d = {nm: nc.dram_tensor(nm, [L, D], BF16, kind="ExternalOutput").ap()
                 for nm in ("dq", "dv", "dy", "dgs", "dyy", "dy2f")}
        dbg_d["dx2"] = nc.dram_tensor("dx2", [L, D], F32, kind="ExternalOutput").ap()
        dbg_d["dro"] = nc.dram_tensor("dro", [L, D], F32, kind="ExternalOutput").ap()
        dtm_d = nc.dram_tensor("dtm", [128, H * 128], BF16, kind="ExternalOutput").ap()
        dsum_d = nc.dram_tensor("dsum", [128, 96], F32, kind="ExternalOutput").ap()
        dcr_d = nc.dram_tensor("dcr", [CH * 16, 1024], F32, kind="ExternalOutput").ap()
    if dbg == "ffn2":
        dst_d = nc.dram_tensor("dstate", [L, D], F32, kind="ExternalOutput").ap()
        dst2_d = nc.dram_tensor("dstate2", [L, D], F32, kind="ExternalOutput").ap()
        dr2_d = nc.dram_tensor("dr2", [L, D], F32, kind="ExternalOutput").ap()
        dxf_d = nc.dram_tensor("dxf", [NG * 128, KC * GROUP], BF16,
                               kind="ExternalOutput").ap()
        dhb_d = nc.dram_tensor("dhb", [NG * NS * (GROUP // 512) * 128, 1024], BF16,
                               kind="ExternalOutput").ap()
        dgs2_d = nc.dram_tensor("dgs2", [NG * NS * (GROUP // 512) * 128, 1024], BF16,
                                kind="ExternalOutput").ap()
        dpa_d = nc.dram_tensor("dpa", [128, 1024], F32, kind="ExternalOutput").ap()
        dwg_d = nc.dram_tensor("dwg", [128, KC * SLICE], BF16, kind="ExternalOutput").ap()

    P = Plan()

    from contextlib import ExitStack
    with ExitStack() as _es:
        wq_bf = _es.enter_context(nc.sbuf_tensor("wq_bf", [128, KC * D], BF16))
        wv_bf = _es.enter_context(nc.sbuf_tensor("wv_bf", [128, KC * D], BF16))
        wg_bf = _es.enter_context(nc.sbuf_tensor("wg_bf", [128, KC * D], BF16))
        wo_bf = _es.enter_context(nc.sbuf_tensor("wo_bf", [128, KC * D], BF16))
        tmat = _es.enter_context(nc.sbuf_tensor("tmat", [128, H * 128], BF16))
        mask16 = _es.enter_context(nc.sbuf_tensor("mask16", [16, 1024], BF16))
        av16f = _es.enter_context(nc.sbuf_tensor("av16f", [16, 128], F32))

        cmat = _es.enter_context(nc.sbuf_tensor("cmat", [128, 16], BF16))
        cmatf = _es.enter_context(nc.sbuf_tensor("cmatf", [128, 16], F32))
        l128d = _es.enter_context(nc.sbuf_tensor("l128d", [16, 16], F32))
        lnl_c = _es.enter_context(nc.sbuf_tensor("lnl_c", [16, 1], F32))
        cr16f = _es.enter_context(nc.sbuf_tensor("cr16f", [16, 1024], F32))

        gam_bc = _es.enter_context(nc.sbuf_tensor("gam_bc", [128, D], BF16))
        bet_bc = _es.enter_context(nc.sbuf_tensor("bet_bc", [128, D], BF16))
        ident = _es.enter_context(nc.sbuf_tensor("ident", [128, 128], BF16))
        carr_f = _es.enter_context(nc.sbuf_tensor("carr_f", [1, D], F32))
        sums = _es.enter_context(nc.sbuf_tensor("sums", [128, 96], F32))
        scr_bf = _es.enter_context(nc.sbuf_tensor("scr_bf", [128, D], BF16))
        dec_f = _es.enter_context(nc.sbuf_tensor("dec_f", [1, 16], F32))
        lnl_f = _es.enter_context(nc.sbuf_tensor("lnl_f", [1, 16], F32))
        ones1 = _es.enter_context(nc.sbuf_tensor("ones1", [1, 128], F32))
        e128 = _es.enter_context(nc.sbuf_tensor("e128", [1, 128], F32))
        lnlbc = _es.enter_context(nc.sbuf_tensor("lnlbc", [128, 16], F32))
        E_f = _es.enter_context(nc.sbuf_tensor("E_f", [128, 128], F32))
        Ttmp0 = _es.enter_context(nc.sbuf_tensor("Ttmp0", [128, 128], F32))
        Ttmp1 = Ttmp0
        idf = Ttmp0
        stage = _es.enter_context(nc.sbuf_tensor("stage", [128, D], F32))
        xf_bf0 = _es.enter_context(nc.sbuf_tensor("xf_bf0", [128, D], BF16))
        xf_bf1 = _es.enter_context(nc.sbuf_tensor("xf_bf1", [128, D], BF16))
        xtok0 = _es.enter_context(nc.sbuf_tensor("xtok0", [128, D], BF16))
        xtok1 = _es.enter_context(nc.sbuf_tensor("xtok1", [128, D], BF16))
        qbf0 = _es.enter_context(nc.sbuf_tensor("qbf0", [128, D], BF16))
        qbf1 = _es.enter_context(nc.sbuf_tensor("qbf1", [128, D], BF16))
        vbf0 = _es.enter_context(nc.sbuf_tensor("vbf0", [128, D], BF16))
        vbf1 = _es.enter_context(nc.sbuf_tensor("vbf1", [128, D], BF16))
        gsb0 = _es.enter_context(nc.sbuf_tensor("gsb0", [128, D], BF16))
        gsb1 = _es.enter_context(nc.sbuf_tensor("gsb1", [128, D], BF16))
        ybf0 = _es.enter_context(nc.sbuf_tensor("ybf0", [128, D], BF16))
        ybf1 = _es.enter_context(nc.sbuf_tensor("ybf1", [128, D], BF16))
        y2f0 = _es.enter_context(nc.sbuf_tensor("y2f0", [128, D], BF16))
        y2f1 = _es.enter_context(nc.sbuf_tensor("y2f1", [128, D], BF16))
        x2bf0 = _es.enter_context(nc.sbuf_tensor("x2bf0", [128, D], BF16))
        x2bf1 = _es.enter_context(nc.sbuf_tensor("x2bf1", [128, D], BF16))
        x2f = _es.enter_context(nc.sbuf_tensor("x2f", [128, KC, GROUP], BF16))
        facc = _es.enter_context(nc.sbuf_tensor("facc", [128, CPG, D], F32))
        wgs0 = _es.enter_context(nc.sbuf_tensor("wgs0", [128, KC * SLICE], BF16))
        wgs1 = _es.enter_context(nc.sbuf_tensor("wgs1", [128, KC * SLICE], BF16))
        wus0 = _es.enter_context(nc.sbuf_tensor("wus0", [128, KC * SLICE], BF16))
        wus1 = _es.enter_context(nc.sbuf_tensor("wus1", [128, KC * SLICE], BF16))
        wds0 = _es.enter_context(nc.sbuf_tensor("wds0", [128, JS * D], BF16))
        wds1 = _es.enter_context(nc.sbuf_tensor("wds1", [128, JS * D], BF16))
        cvt32a = _es.enter_context(nc.sbuf_tensor("cvt32a", [128, 512], I32))
        cvt32b = _es.enter_context(nc.sbuf_tensor("cvt32b", [128, 512], I32))
        cvtbfa = _es.enter_context(nc.sbuf_tensor("cvtbfa", [128, 512], BF16))
        cvtbfb = _es.enter_context(nc.sbuf_tensor("cvtbfb", [128, 512], BF16))
        gs20 = _es.enter_context(nc.sbuf_tensor("gs20", [128, 1024], BF16))
        gs21 = gs20
        hbf0 = _es.enter_context(nc.sbuf_tensor("hbf0", [128, 1024], BF16))
        hbf1 = _es.enter_context(nc.sbuf_tensor("hbf1", [128, 1024], BF16))
        PA = _es.enter_context(nc.psum_tensor("PA", [128, 1024], F32))
        PB = _es.enter_context(nc.psum_tensor("PB", [128, 1024], F32))
        PC = _es.enter_context(nc.psum_tensor("PC", [128, 1024], F32))
        PD = _es.enter_context(nc.psum_tensor("PD", [128, 1024], F32))
        semT = _es.enter_context(nc.semaphore("semT"))
        semV = _es.enter_context(nc.semaphore("semV"))
        semS = _es.enter_context(nc.semaphore("semS"))
        semG = _es.enter_context(nc.semaphore("semG"))
        DCH = ["stage", "carr", "dec", "wgs0", "wgs1", "wus0", "wus1",
               "wds0", "wds1", "cvt0", "cvt1", "cvs0", "cvs1"] + \
              ["fac%d" % i for i in range(8)]
        DSEM = {c: _es.enter_context(nc.semaphore("semD" + c)) for c in DCH}
        semF = _es.enter_context(nc.semaphore("semF"))
        semO = _es.enter_context(nc.semaphore("semO"))
        block = _es.enter_context(nc.Block())
        SEM = dict(T=semT, V=semV, S=semS, G=semG, F=semF, O=semO)
        for c in DCH:
            SEM["D" + c] = DSEM[c]
        xf_bf = [xf_bf0, xf_bf1]
        xtok = [xtok0, xtok1]
        qbf = [qbf0, qbf1]
        vbf = [vbf0, vbf1]
        gsb = [gsb0, gsb1]
        ybf = [ybf0, ybf1]
        yy = ybf
        y2f = [y2f0, y2f1]
        x2bf = [x2bf0, x2bf1]
        wgs = [wgs0, wgs1]
        wus = [wus0, wus1]
        wds = [wds0, wds1]
        cvt32 = [cvt32a, cvt32b]
        cvtbf = [cvtbfa, cvtbfb]
        gs2 = [gs20, gs21]
        hbf = [hbf0, hbf1]


        def sl(i):
            return sums[:, i : i + 1]

        def ydma(fn, waits=(), ch="stage"):
            return P.op("Y", fn, waits, inc=("D" + ch, 16))

        def ydma_flush():
            pass

        def vnop():
            P.op("V", lambda v: v.engine_nop())

        def vchain(fn, waits=()):
            return P.op("V", fn, waits, inc="V")

        def nr_rsqrt(veps_i, rstd_i, tmp_i, ev0, fuse_tail=False):
            ve = sl(veps_i)
            rs = sl(rstd_i)
            tm = sl(tmp_i)
            vh = sl(tmp_i + 1)
            e1 = vchain(lambda v, ve=ve, tm=tm: v.tensor_scalar(
                tm.bitcast(I32), ve.bitcast(I32), 1, None, ALU.arith_shift_right), [ev0])
            vchain(lambda v, ve=ve, vh=vh: v.tensor_scalar(vh, ve, -0.5, None, ALU.mult), [ev0])
            e = vchain(lambda v, rs=rs, tm=tm: v.tensor_scalar(
                rs.bitcast(I32), tm.bitcast(I32), MAGIC, -1, ALU.subtract, ALU.mult), [e1])
            for _ in range(1):
                e = vchain(lambda v, rs=rs, tm=tm: v.tensor_tensor(tm, rs, rs, ALU.mult), [e])
                e = vchain(lambda v, tm=tm, vh=vh: v.tensor_scalar(
                    tm, tm, vh, 1.5, ALU.mult, ALU.add), [e])
                if not fuse_tail:
                    e = vchain(lambda v, rs=rs, tm=tm: v.tensor_tensor(
                        rs, rs, tm, ALU.mult), [e])
            return e

        e_dec = ydma(lambda s: s.dma_start(out=dec_f[0:1, 0:16], in_=dec_d[None, :]),
                     ch="dec")
        e_gam = ydma(lambda s: s.dma_start(out=carr_f[0:1, :], in_=gam_d[None, :]),
                     ch="carr")

        ge = None
        def gop(fn):
            nonlocal_ns["ge"] = P.op("G", fn, [nonlocal_ns["ge"]], inc="G")
        nonlocal_ns = {"ge": None}
        gop(lambda g: g.iota(E_f[:], pattern=[[1, 128]], base=0,
                             channel_multiplier=-1, allow_small_or_imprecise_dtypes=True))
        gop(lambda g: g.affine_select(E_f[:], E_f[:], pattern=[[1, 128]],
                                      compare_op=ALU.is_ge, fill=1e30, base=0,
                                      channel_multiplier=-1))
        gop(lambda g: g.memset(idf[:], 1.0))
        gop(lambda g: g.affine_select(idf[:], idf[:], pattern=[[1, 128]],
                                      compare_op=ALU.is_equal, fill=0.0, base=0,
                                      channel_multiplier=-1))
        gop(lambda g: g.memset(ones1[:], 1.0))
        gop(lambda g: g.iota(e128[0:1, :], pattern=[[1, 128]], base=1,
                             channel_multiplier=0,
                             allow_small_or_imprecise_dtypes=True))
        gop(lambda g: g.memset(mask16[:], 1.0))
        gop(lambda g: g.affine_select(mask16[:], mask16[:], pattern=[[1, 1024]],
                                      compare_op=ALU.is_ge, fill=0.0, base=0,
                                      channel_multiplier=-64))
        gop(lambda g: g.affine_select(mask16[:], mask16[:], pattern=[[-1, 1024]],
                                      compare_op=ALU.is_ge, fill=0.0, base=63,
                                      channel_multiplier=64))
        gop(lambda g: g.iota(av16f[:], pattern=[[1, 128]], base=1,
                             channel_multiplier=0,
                             allow_small_or_imprecise_dtypes=True))
        gop(lambda g: g.iota(cmatf[:], pattern=[[0, 16]], base=127,
                             channel_multiplier=-1,
                             allow_small_or_imprecise_dtypes=True))
        gop(lambda g: g.memset(l128d[:], 1.0))
        gop(lambda g: g.affine_select(l128d[:], l128d[:], pattern=[[1, 16]],
                                      compare_op=ALU.is_equal, fill=0.0, base=0,
                                      channel_multiplier=-1))
        gop(lambda g: g.memset(cr16f[:], 0.0))
        e_gp = P.op("G", lambda g: g.nop(), [nonlocal_ns["ge"]], inc="G")

        e_id = P.op("S", lambda s: s.copy(ident[:], idf[:]), [e_gp], inc="S")
        e_lnl = P.op("S", lambda s: s.activation(lnl_f[0:1, 0:16], dec_f[0:1, 0:16], AF.Ln),
                     [e_dec], inc="S")
        e_mml = P.op("T", lambda t: t.matmul(PA[:, 0:16], ones1[0:1, :], lnl_f[0:1, 0:16],
                                             start=True, stop=True), [e_lnl, e_gp], inc="T")
        e_lnlbc = P.op("V", lambda v: v.tensor_copy(lnlbc[:], PA[:, 0:16]), [e_mml], inc="V")
        P.op("T", lambda t: t.matmul(PB[:, 0:512], ones1[0:1, :], carr_f[0:1, 0:512],
                                     start=True, stop=True), [e_gam, e_gp])
        e_mg1 = P.op("T", lambda t: t.matmul(PB[:, 512:1024], ones1[0:1, :],
                                             carr_f[0:1, 512:1024], start=True, stop=True),
                     inc="T")
        e_gbc = P.op("S", lambda s: s.copy(gam_bc[:], PB[:]), [e_mg1], inc="S")
        e_bet = ydma(lambda s: s.dma_start(out=carr_f[0:1, :], in_=bet_d[None, :]),
                     [e_gbc], ch="carr")
        P.op("T", lambda t: t.matmul(PC[:, 0:512], ones1[0:1, :], carr_f[0:1, 0:512],
                                     start=True, stop=True), [e_bet])
        e_mb1 = P.op("T", lambda t: t.matmul(PC[:, 512:1024], ones1[0:1, :],
                                             carr_f[0:1, 512:1024], start=True, stop=True),
                     inc="T")
        e_bbc = P.op("S", lambda s: s.copy(bet_bc[:], PC[:]), [e_mb1], inc="S")
        e_cz = P.op("V", lambda v: v.engine_nop(), [e_mb1], inc="V")

        Tt = [Ttmp0, Ttmp1]
        ev_texp = {}
        for h in range(H):
            tp = Tt[h % 2]
            w = [e_lnlbc, e_id]
            if h >= 1:
                w.append(ev_texp[h - 1])
            ev_tmul = P.op("V", lambda v, tp=tp, h=h: v.tensor_scalar(
                tp[:], E_f[:], lnlbc[:, h : h + 1], None, ALU.mult), w, inc="V")
            ev_texp[h] = P.op("S", lambda s, tp=tp, h=h: s.activation(
                tmat[:, h * 128 : (h + 1) * 128], tp[:], AF.Exp), [ev_tmul], inc="S")
        last_texp = ev_texp[H - 1]
        e_mmc = P.op("T", lambda t: t.matmul(PD[0:16, 0:1], lnl_f[0:1, 0:16],
                                             ones1[0:1, 0:1], start=True, stop=True),
                     [e_lnl, e_gp], inc="T")
        e_lnc = P.op("V", lambda v: v.tensor_copy(lnl_c[:], PD[0:16, 0:1]),
                     [e_mmc], inc="V")
        e_av1 = P.op("V", lambda v: v.tensor_scalar(av16f[:], av16f[:], lnl_c[:],
                                                    None, ALU.mult), [e_lnc], inc="V")
        e_av = P.op("S", lambda s: s.activation(av16f[:], av16f[:], AF.Exp),
                    [e_av1], inc="S")
        e_cm1 = P.op("V", lambda v: v.tensor_tensor(cmatf[:], cmatf[:], lnlbc[:, 0:16],
                                                    ALU.mult), [e_lnlbc, e_gp], inc="V")
        e_cm = P.op("S", lambda s: s.activation(cmat[:], cmatf[:], AF.Exp),
                    [e_cm1], inc="S")
        e_l1 = P.op("V", lambda v: v.tensor_scalar(lnl_c[:], lnl_c[:], 128.0, None,
                                                   ALU.mult), [e_av1], inc="V")
        e_l2 = P.op("S", lambda s: s.activation(lnl_c[:], lnl_c[:], AF.Exp),
                    [e_l1], inc="S")
        e_l128 = P.op("V", lambda v: v.tensor_scalar(l128d[:], l128d[:], lnl_c[:],
                                                     None, ALU.mult), [e_l2], inc="V")

        wcast = {}
        wsrc = [(wqT_d, wq_bf), (wvT_d, wv_bf), (wgT_d, wg_bf), (woT_d, wo_bf)]
        NFS = min(CPG, 8)
        for wi, (wd, wb) in enumerate(wsrc):
            for k in range(KC):
                i = wi * KC + k
                slot = i % NFS
                w = [wcast[i - NFS]] if i >= NFS else []
                wl = ydma(lambda s, wd=wd, k=k, slot=slot: s.dma_start(
                    out=facc[:, slot, :], in_=wd[k * 128 : (k + 1) * 128, :]),
                    w, ch="fac%d" % slot)
                wcast[i] = P.op("S", lambda s, wb=wb, k=k, slot=slot: s.copy(
                    wb[:, k * D : (k + 1) * D], facc[:, slot, :]), [wl], inc="S")
        e_wdone = wcast[4 * KC - 1]

        E_xt, E_xf, E_xc = {}, {}, {}
        E_tq, E_tv, E_tg, E_tsc, E_ty2, E_to, E_tx2 = {}, {}, {}, {}, {}, {}, {}
        E_qe, E_ve, E_gs, E_yl, E_y2, E_y2e = {}, {}, {}, {}, {}, {}
        E_r, E_x2, E_x2b, E_x2e, E_st = {}, {}, {}, {}, {}
        E_tce = {}
        E_crf, E_crb = {}, {}
        E_nr1, E_nr2, E_sq2, E_se, E_y0, E_hbd, E_gsd, E_s2 = {}, {}, {}, {}, {}, {}, {}, {}
        scr_prev = [None]
        E_fl, E_fc = {}, {}
        E_gate, E_up, E_hT, E_dn, E_sil, E_hb, E_hf, E_fa = {}, {}, {}, {}, {}, {}, {}, {}
        E_vffn = {}

        def slots(par):
            b = par * 24
            return dict(sy1=b + 0, sy2=b + 1, nm1=b + 2, tt1=b + 3, ve1=b + 4,
                        rs1=b + 5, tn1=b + 6, nmr=b + 8, sr1=b + 9, sr2=b + 10,
                        rsf=b + 17,
                        nm2=b + 11, tt2=b + 12, ve2=b + 13, rs2=b + 14, tn2=b + 15)

        def ln_stats(s1, s2, nm, tt_, ve, waits):
            e1 = vchain(lambda v: v.tensor_scalar(sl(nm), sl(s1), -1.0 / D, None, ALU.mult),
                        waits)
            vchain(lambda v: v.tensor_scalar(sl(ve), sl(s2), 1.0 / D, LN_EPS,
                                             ALU.mult, ALU.add))
            e2 = vchain(lambda v: v.tensor_tensor(sl(tt_), sl(nm), sl(nm), ALU.mult), [e1])
            return vchain(lambda v: v.tensor_tensor(sl(ve), sl(ve), sl(tt_), ALU.subtract),
                          [e2])

        def x_load(gc):
            par = gc % 2
            w = [E_xt.get(gc - 1)] if gc >= 1 else [e_wdone]
            t0 = gc * 128
            cf1 = ydma(lambda s, t0=t0: s.dma_start(
                out=stage[:].rearrange("p (k t) -> p k t", k=KC),
                in_=xT_d.rearrange("(k p) t -> p k t", p=128)[:, :, t0 : t0 + 128]), w)
            E_xc[gc] = P.op("S", lambda s, par=par: s.copy(xf_bf[par][:], stage[:]),
                            [cf1, E_tg.get(gc - 2)], inc="S")
            cf2 = ydma(lambda s, t0=t0: s.dma_start(
                out=stage[:], in_=x_d[t0 : t0 + 128, :]), [E_xc[gc]])
            E_xt[gc] = P.op("S", lambda s, par=par: s.copy(xtok[par][:], stage[:]),
                            [cf2, E_r.get(gc - 2)], inc="S")

        def s_xfcast(gc):
            pass

        def t_qvg(gc):
            par = gc % 2
            g = gc // CPG
            first_w = [E_xc[gc], E_r.get(gc - 2), E_qe.get(gc - 1)]
            if gc == 0:
                first_w += [e_lnlbc, e_gbc, e_bbc]
            if gc % CPG == 0 and g >= 1:
                first_w.append(E_vffn.get(g - 1))
            for (dst, wb, tab, extra) in (
                (PA, wq_bf, E_tq, []),
                (PC, wg_bf, E_tg, [E_gs.get(gc - 1)]),
                (PB, wv_bf, E_tv, [E_crf.get(gc - 1), E_x2e.get(gc - 3)]),
            ):
                w = first_w + extra
                first_w = []
                ev = None
                for half in range(2):
                    for k in range(KC):
                        ev = P.op("T", lambda t, dst=dst, wb=wb, par=par, k=k, half=half:
                                  t.matmul(
                                      dst[:, half * 512 : (half + 1) * 512],
                                      xf_bf[par][:, k * 128 : (k + 1) * 128],
                                      wb[:, k * D + half * 512 : k * D + (half + 1) * 512],
                                      start=(k == 0), stop=(k == KC - 1)),
                                  w if (half == 0 and k == 0) else (),
                                  inc="T" if (half == 1 and k == KC - 1) else None)
                tab[gc] = ev

        def s_qv_evac(gc):
            par = gc % 2
            E_qe[gc] = P.op("S", lambda s, par=par: s.copy(qbf[par][:], PA[:]),
                            [E_tq[gc], E_s2.get(gc - 2)], inc="S")
            E_ve[gc] = P.op("S", lambda s, par=par: s.copy(vbf[par][:], PB[:]),
                            [E_tv[gc], E_y0.get(gc - 2)], inc="S")

        def s_silu(gc):
            par = gc % 2
            E_gs[gc] = P.op("S", lambda s, par=par: s.activation(gsb[par][:], PC[:], AF.Silu),
                            [E_tg[gc]], inc="S")

        def t_scan(gc):
            par = gc % 2
            w = [E_ve[gc], e_av, last_texp, e_cz, e_l128, e_cm,
                 E_crf.get(gc - 1), E_y2e.get(gc - 1)]
            P.op("T", lambda t: t.matmul(PD[:, 0:512], av16f[:], cr16f[:, 0:512],
                                         start=True, stop=True), w)
            P.op("T", lambda t: t.matmul(PD[:, 512:1024], av16f[:], cr16f[:, 512:1024],
                                         start=True, stop=True))
            ev = None
            for h in range(H):
                ev = P.op("T", lambda t, par=par, h=h: t.matmul(
                    PD[:, h * HD : (h + 1) * HD], tmat[:, h * 128 : (h + 1) * 128],
                    vbf[par][:, h * HD : (h + 1) * HD], start=False, stop=True,
                    skip_group_check=True),
                    inc="T" if h == H - 1 else None)
            E_tsc[gc] = ev
            wce = [E_x2e.get(gc - 2), E_crf.get(gc - 1)]
            P.op("T", lambda t: t.matmul(PB[0:16, 0:512], l128d[:], cr16f[:, 0:512],
                                         start=True, stop=False), wce)
            P.op("T", lambda t: t.matmul(PB[0:16, 512:1024], l128d[:],
                                         cr16f[:, 512:1024], start=True, stop=False))
            P.op("T", lambda t, par=par: t.matmul(PB[0:16, 0:512], cmat[:],
                                                  vbf[par][:, 0:512],
                                                  start=False, stop=True))
            E_tce[gc] = P.op("T", lambda t, par=par: t.matmul(
                PB[0:16, 512:1024], cmat[:], vbf[par][:, 512:1024],
                start=False, stop=True), inc="T")

        def v_carr16(gc):
            P.op("V", lambda v: v.tensor_tensor(
                cr16f[:], mask16[:], PB[0:16, :], ALU.mult), [E_tce[gc]])
            P.op("V", lambda v: v.engine_nop())
            E_crf[gc] = P.op("V", lambda v: v.engine_nop(), inc="V")

        def v_y_ln1(gc):
            par = gc % 2
            sm = slots(par)
            E_se[gc] = P.op("S", lambda s, par=par: s.copy(vbf[par][:], PD[:]),
                            [E_tce[gc]], inc="S")
            wy = [E_se[gc], E_qe[gc]]
            if dbg == "ffn2":
                evs = P.op("S", lambda s: s.copy(scr_bf[:], PD[:]), [E_tsc[gc]], inc="S")
                wy.append(P.op("G", lambda gp, gc=gc: gp.dma_start(
                    out=dst_d[gc * 128 : (gc + 1) * 128, :], in_=scr_bf[:]),
                    [evs], inc=("G", 16)))
            e_y0 = P.op("V", lambda v, par=par, sm=sm: v.scalar_tensor_tensor(
                ybf[par][:], qbf[par][:], 1.0, vbf[par][:], ALU.mult, ALU.mult,
                accum_out=sl(sm["sy1"])), wy, inc="V")
            E_y0[gc] = e_y0
            w_s2 = [e_y0]
            if dbg == "ffn2":
                t0p = gc * 128
                evs2 = P.op("S", lambda s: s.copy(scr_bf[:], PD[:]), [e_y0], inc="S")
                P.op("G", lambda gp, t0p=t0p: gp.dma_start(
                    out=dst2_d[t0p : t0p + 128, :], in_=scr_bf[:]),
                    [evs2], inc=("G", 16))
                w_s2.append(P.op("G", lambda gp, par=par, t0p=t0p: gp.dma_start(
                    out=dbg_d["dy"][t0p : t0p + 128, :], in_=ybf[par][:]),
                    [e_y0], inc=("G", 16)))
            e_s2 = P.op("V", lambda v, par=par, sm=sm: v.scalar_tensor_tensor(
                qbf[par][:], ybf[par][:], 1.0, ybf[par][:], ALU.mult, ALU.mult,
                accum_out=sl(sm["sy2"])), w_s2, inc="V")
            E_s2[gc] = e_s2
            e = ln_stats(sm["sy1"], sm["sy2"], sm["nm1"], sm["tt1"], sm["ve1"], [e_s2])
            e = nr_rsqrt(sm["ve1"], sm["rs1"], sm["tn1"], e, fuse_tail=True)
            P.op("V", lambda v, sm=sm: v.scalar_tensor_tensor(
                sl(sm["nmr"]), sl(sm["rs1"]), sl(sm["nm1"]), sl(sm["tn1"]),
                ALU.mult, ALU.mult), [e], inc="V")
            E_nr1[gc] = P.op("V", lambda v, sm=sm: v.tensor_tensor(
                sl(sm["rsf"]), sl(sm["rs1"]), sl(sm["tn1"]), ALU.mult), [e], inc="V")

        def s_yln(gc):
            par = gc % 2
            sm = slots(par)
            E_yl[gc] = P.op("S", lambda s, par=par, sm=sm: s.activation(
                yy[par][:], ybf[par][:], AF.Identity, bias=sl(sm["nmr"]), scale=sl(sm["rsf"])),
                [E_nr1[gc]], inc="S")

        def v_y2(gc):
            par = gc % 2
            e1 = P.op("V", lambda v, par=par: v.tensor_tensor(
                yy[par][:], yy[par][:], gam_bc[:], ALU.mult), [E_yl[gc]], inc="V")
            e2 = P.op("V", lambda v, par=par: v.tensor_tensor(
                yy[par][:], yy[par][:], bet_bc[:], ALU.add), [e1], inc="V")
            E_y2[gc] = P.op("V", lambda v, par=par: v.tensor_tensor(
                yy[par][:], yy[par][:], gsb[par][:], ALU.mult), [E_gs[gc], e2], inc="V")

        def t_y2T(gc):
            par = gc % 2
            w = [E_y2[gc], E_se[gc]]
            if dbg == "ffn2":
                t0p = gc * 128
                w.append(P.op("G", lambda gp, par=par, t0p=t0p: gp.dma_start(
                    out=dbg_d["dyy"][t0p : t0p + 128, :], in_=yy[par][:]),
                    [E_y2[gc]], inc=("G", 16)))
            ev = None
            for k in range(KC):
                ev = P.op("T", lambda t, par=par, k=k: t.matmul(
                    PD[:, k * 128 : (k + 1) * 128], yy[par][:, k * 128 : (k + 1) * 128],
                    ident[:], start=True, stop=True), w if k == 0 else (),
                    inc="T" if k == KC - 1 else None)
            E_ty2[gc] = ev

        def s_y2f(gc):
            par = gc % 2
            E_y2e[gc] = P.op("S", lambda s, par=par: s.copy(y2f[par][:], PD[:]),
                             [E_ty2[gc]], inc="S")

        def t_o(gc):
            par = gc % 2
            w = [E_y2e[gc], E_qe.get(gc + 1), E_r.get(gc - 1)]
            ev = None
            for half in range(2):
                for k in range(KC):
                    ev = P.op("T", lambda t, par=par, k=k, half=half: t.matmul(
                        PA[:, half * 512 : (half + 1) * 512],
                        y2f[par][:, k * 128 : (k + 1) * 128],
                        wo_bf[:, k * D + half * 512 : k * D + (half + 1) * 512],
                        start=(k == 0), stop=(k == KC - 1)),
                        w if (half == 0 and k == 0) else (),
                        inc="T" if (half == 1 and k == KC - 1) else None)
            E_to[gc] = ev

        def v_ln2(gc):
            par = gc % 2
            cg = gc % CPG
            g = gc // CPG
            sm = slots(par)
            w = [E_to[gc], E_xt[gc]]
            if g >= 1:
                w.append(("O", 16 * g * CPG))
            E_r[gc] = P.op("V", lambda v, par=par, cg=cg, sm=sm: v.scalar_tensor_tensor(
                facc[:, cg, :], xtok[par][:], 1.0, PA[:], ALU.mult, ALU.add,
                accum_out=sl(sm["sr1"])), w, inc="V")

        def s_sq2(gc):
            cg = gc % CPG
            sm = slots(gc % 2)
            E_sq2[gc] = P.op("S", lambda s, cg=cg, sm=sm: s.activation(
                scr_bf[:], facc[:, cg, :], AF.Square, accum_out=sl(sm["sr2"])),
                [E_r[gc], scr_prev[0]], inc="S")
            scr_prev[0] = E_sq2[gc]

        def v_x2(gc):
            cg = gc % CPG
            sm = slots(gc % 2)
            w2 = [E_sq2[gc]]
            if dbg in ("probe", "ffn2"):
                cg_ = gc % CPG
                t0_ = gc * 128
                ev = P.op("G", lambda gp, cg_=cg_, t0_=t0_: gp.dma_start(
                    out=dbg_d["dro"][t0_ : t0_ + 128, :], in_=facc[:, cg_, :]),
                    [E_sq2[gc]], inc=("G", 16))
                w2.append(ev)
            e = ln_stats(sm["sr1"], sm["sr2"], sm["nm2"], sm["tt2"], sm["ve2"], w2)
            e = nr_rsqrt(sm["ve2"], sm["rs2"], sm["tn2"], e)
            E_x2[gc] = P.op("V", lambda v, cg=cg, sm=sm: v.tensor_scalar(
                facc[:, cg, :], facc[:, cg, :], sl(sm["nm2"]), sl(sm["rs2"]),
                ALU.add, ALU.mult), [e], inc="V")

        def s_x2b(gc):
            par = gc % 2
            cg = gc % CPG
            E_x2b[gc] = P.op("S", lambda s, par=par, cg=cg: s.copy(
                x2bf[par][:], facc[:, cg, :]), [E_x2[gc]], inc="S")

        def t_x2F(gc):
            par = gc % 2
            w = [E_x2b[gc], E_ve.get(gc + 2), E_x2e.get(gc - 1), E_crf.get(gc),
                 E_crf.get(gc + 1)]
            ev = None
            for k in range(KC):
                ev = P.op("T", lambda t, par=par, k=k: t.matmul(
                    PB[:, k * 128 : (k + 1) * 128], x2bf[par][:, k * 128 : (k + 1) * 128],
                    ident[:], start=True, stop=True), w if k == 0 else (),
                    inc="T" if k == KC - 1 else None)
            E_tx2[gc] = ev

        def s_x2e(gc):
            cg = gc % CPG
            E_x2e[gc] = P.op("S", lambda s, cg=cg: s.copy(
                x2f[:, :, cg * 128 : (cg + 1) * 128],
                PB[:].rearrange("p (k t) -> p k t", k=KC)), [E_tx2[gc]], inc="S")

        wgq_r = wgq_d.rearrange("(k p) h -> p k h", p=128)
        wuq_r = wuq_d.rearrange("(k p) h -> p k h", p=128)
        wpump = []

        conv_pieces = []
        for cc in range(DH // 1024):
            for k in range(KC):
                for hh in range(2):
                    conv_pieces.append(("g", wgq_d, wgb_d, k * 128, cc * 1024 + hh * 512))
            for k in range(KC):
                for hh in range(2):
                    conv_pieces.append(("u", wuq_d, wub_d, k * 128, cc * 1024 + hh * 512))
            for p in range(8):
                for hh in range(2):
                    conv_pieces.append(("d", wdq_d, wdb_d, cc * 1024 + p * 128, hh * 512))
        NCONV = len(conv_pieces)
        conv_state = {"n": 0}
        E_cvc = {}

        def conv_pump(nsteps):
            for _ in range(nsteps):
                p = conv_state["n"]
                if p >= NCONV:
                    return
                conv_state["n"] = p + 1
                kind, srcd, dstd, r0, c0 = conv_pieces[p]
                par = p % 2
                cf = ydma(lambda s, srcd=srcd, r0=r0, c0=c0, par=par: s.dma_start(
                    out=cvt32[par][:], in_=srcd[r0 : r0 + 128, c0 : c0 + 512]),
                    [E_cvc.get(p - 2)], ch="cvt%d" % par)
                E_cvc[p] = P.op("G", lambda g, par=par: g.tensor_copy(
                    cvtbf[par][:], cvt32[par][:]),
                    [cf, ("Dcvs%d" % par, 16 * (p // 2))], inc="G")
                P.op("G", lambda g, dstd=dstd, r0=r0, c0=c0, par=par: g.dma_start(
                    out=dstd[r0 : r0 + 128, c0 : c0 + 512], in_=cvtbf[par][:]),
                    [E_cvc[p]], inc=("Dcvs%d" % par, 16))

        def conv_block_done(s_):
            blk = min(s_ * SLICE // 1024, DH // 1024 - 1)
            return [("Dcvs0", 16 * 24 * (blk + 1)), ("Dcvs1", 16 * 24 * (blk + 1))]

        wgb_r = wgb_d.rearrange("(k p) h -> p k h", p=128)
        wub_r = wub_d.rearrange("(k p) h -> p k h", p=128)

        def ffn_wload(gsl):
            par = gsl % 2
            g_ = gsl // NS
            s_ = gsl % NS
            s0 = s_ * SLICE
            lastit = lambda g2: (g2 + 1) * (GROUP // 512) - 1

            def st_g(x=None):
                w = [E_gate.get((lastit(gsl - 2), JS - 1))]
                if g_ == 0:
                    w.extend(conv_block_done(s_))
                E_fc[("gc", gsl)] = ydma(lambda s: s.dma_start(
                    out=wgs[par][:].rearrange("p (k h) -> p k h", k=KC),
                    in_=wgb_r[:, :, s0 : s0 + SLICE]), w, ch="wgs%d" % par)

            def st_u(x=None):
                w = [E_up.get((lastit(gsl - 2), JS - 1))]
                E_fc[("uc", gsl)] = ydma(lambda s: s.dma_start(
                    out=wus[par][:].rearrange("p (k h) -> p k h", k=KC),
                    in_=wub_r[:, :, s0 : s0 + SLICE]), w, ch="wus%d" % par)

            def st_d(x=None):
                w = [E_dn.get(lastit(gsl - 2))]
                E_fc[("dc", gsl)] = ydma(lambda s: s.dma_start(
                    out=wds[par][:].rearrange("p (j d) -> p j d", j=JS),
                    in_=wdb_d[s0 : s0 + SLICE, :].rearrange("(j p) d -> p j d", p=128)),
                    w, ch="wds%d" % par)

            wpump.extend([st_g, st_u, st_d])

        PUMP_N = max(-(-3 // max(GROUP // 512, 1)), -(-3 // CPG), 1) + 1

        def pump():
            for _ in range(PUMP_N):
                if wpump:
                    wpump.pop(0)()

        NB = GROUP // 512
        TB = 512

        def ffn_gu(fi):
            sb = fi // NB
            g = sb // NS
            b = fi % NB
            par = sb % 2
            t0 = b * TB
            w = [E_fc[("gc", sb)], E_sil.get((fi - 1, 0)), E_x2e[g * CPG + CPG - 1]]
            for j in range(JS):
                if j == 1:
                    w = [E_sil.get((fi - 1, 1))]
                for k in range(KC):
                    ev = P.op("T", lambda t, j=j, k=k, par=par, t0=t0: t.matmul(
                        PA[:, j * 512 : (j + 1) * 512],
                        wgs[par][:, k * SLICE + j * 128 : k * SLICE + (j + 1) * 128],
                        x2f[:, k, t0 : t0 + TB],
                        start=(k == 0), stop=(k == KC - 1)),
                        w if k == 0 else (),
                        inc="T" if k == KC - 1 else None)
                E_gate[(fi, j)] = ev
            for j in range(JS):
                w = [E_fc[("uc", sb)], E_hb.get((fi - 1, j))]
                for k in range(KC):
                    ev = P.op("T", lambda t, j=j, k=k, par=par, t0=t0: t.matmul(
                        PB[:, j * 512 : (j + 1) * 512],
                        wus[par][:, k * SLICE + j * 128 : k * SLICE + (j + 1) * 128],
                        x2f[:, k, t0 : t0 + TB],
                        start=(k == 0), stop=(k == KC - 1)),
                        w if k == 0 else (),
                        inc="T" if k == KC - 1 else None)
                E_up[(fi, j)] = ev

        def ffn_sil(fi):
            ip = fi % 2
            for j in range(JS):
                E_sil[(fi, j)] = P.op("S", lambda s, ip=ip, j=j: s.activation(
                    gs2[ip][:, j * 512 : (j + 1) * 512],
                    PA[:, j * 512 : (j + 1) * 512], AF.Silu, scale=QS),
                    [E_gate[(fi, j)], E_hb.get((fi - 1, j))], inc="S")
            for j in range(JS):
                E_hb[(fi, j)] = P.op("V", lambda v, ip=ip, j=j: v.scalar_tensor_tensor(
                    hbf[ip][:, j * 512 : (j + 1) * 512],
                    gs2[ip][:, j * 512 : (j + 1) * 512], QS,
                    PB[:, j * 512 : (j + 1) * 512], ALU.mult, ALU.mult),
                    [E_sil[(fi, j)], E_up[(fi, j)], E_dn.get(fi - 2)], inc="V")

        def ffn_down(g, s_, b, fi):
            sb = g * NS + s_
            par = sb % 2
            ip = fi % 2
            base = g * CPG
            last = s_ == NS - 1
            e_last = {}
            for tc in range(4):
                cg = b * 4 + tc
                bank = PC if tc % 2 == 0 else PD
                w = [E_hb[(fi, 0)], E_hb[(fi, 1)],
                     E_fa.get((fi - 1, tc - 2 + 4 * ((tc - 2) < 0))),
                     E_fc[("dc", sb)]]
                if tc >= 2:
                    w = [E_fa.get((fi, tc - 2))]
                ev = None
                for j in range(JS):
                    for half in range(2):
                        ev = P.op("T", lambda t, j=j, half=half, par=par, ip=ip,
                                  tc=tc, bank=bank: t.matmul(
                            bank[:, half * 512 : (half + 1) * 512],
                            hbf[ip][:, j * 512 + tc * 128 : j * 512 + (tc + 1) * 128],
                            wds[par][:, j * D + half * 512 : j * D + (half + 1) * 512],
                            start=(j == 0), stop=(j == JS - 1)),
                            w if (j == 0 and half == 0) else (),
                            inc="T" if (j == JS - 1 and half == 1) else None)
                E_dntc = ev
                acc = sl(48 + cg) if last else None
                E_fa[(fi, tc)] = P.op("V", lambda v, cg=cg, acc=acc, bank=bank:
                                      v.scalar_tensor_tensor(
                    facc[:, cg, :], bank[:], QS, facc[:, cg, :], ALU.mult, ALU.add,
                    accum_out=acc), [E_dntc], inc="V")
                e_last[cg] = E_fa[(fi, tc)]
            E_dn[fi] = ev
            return e_last

        def ln3_out(gc, e_lastadd):
            cg = gc % CPG
            par = gc % 2
            s1 = 48 + cg
            s2 = 56 + cg
            nm = 64 + par * 8
            tt_ = 65 + par * 8
            ve = 66 + par * 8
            rs = 67 + par * 8
            tn = 68 + par * 8
            E_s3 = P.op("S", lambda s, cg=cg, s2=s2: s.activation(
                scr_bf[:], facc[:, cg, :], AF.Square, accum_out=sl(s2)),
                [e_lastadd, scr_prev[0]], inc="S")
            scr_prev[0] = E_s3
            wl3 = [E_s3]
            if dbg == "ffn2":
                t0p = gc * 128
                g_p = gc // CPG
                evd = P.op("G", lambda gp, cg=cg, t0p=t0p: gp.dma_start(
                    out=dr2_d[t0p : t0p + 128, :], in_=facc[:, cg, :]),
                    [e_lastadd], inc=("G", 16))
                wl3.append(evd)
                if cg == CPG - 1:
                    P.op("G", lambda gp, g_p=g_p: gp.dma_start(
                        out=dxf_d[g_p * 128 : (g_p + 1) * 128, :],
                        in_=x2f[:].rearrange("p a b -> p (a b)")), inc=("G", 16))
            e = ln_stats(s1, s2, nm, tt_, ve, wl3)
            e = nr_rsqrt(ve, rs, tn, e)
            e_x3 = P.op("V", lambda v, cg=cg, nm=nm, rs=rs: v.tensor_scalar(
                facc[:, cg, :], facc[:, cg, :], sl(nm), sl(rs), ALU.add, ALU.mult),
                [e], inc="V")
            t0 = gc * 128
            E_st[gc] = P.op("G", lambda gp, cg=cg, t0=t0: gp.dma_start(
                out=y_d[t0 : t0 + 128, :], in_=facc[:, cg, :]), [e_x3], inc=("O", 16))

        wload_done = set()

        def wload_once(gsl):
            if gsl < NG * NS and gsl not in wload_done:
                wload_done.add(gsl)
                ffn_wload(gsl)

        CPUMP = -(-NCONV // CPG)
        for g in range(NG):
            base = g * CPG
            x_load(base)
            s_xfcast(base)
            x_load(base + 1)
            if g >= 1:
                wload_once(g * NS + 0)
                wload_once(g * NS + 1)
            t_qvg(base)
            s_qv_evac(base)
            s_silu(base)
            for cg in range(CPG):
                gc = base + cg
                nxt = gc + 1
                t_scan(gc)
                v_carr16(gc)
                if dbg == "probe":
                    P.op("G", lambda gp, gc=gc: gp.dma_start(
                        out=dcr_d[gc * 16 : (gc + 1) * 16, :], in_=cr16f[:]),
                        [E_crf[gc]], inc=("O", 16))
                if g == 0:
                    conv_pump(CPUMP)
                pump()
                v_y_ln1(gc)
                s_yln(gc)
                v_y2(gc)
                if cg + 1 < CPG:
                    s_xfcast(nxt)
                    t_qvg(nxt)
                    s_qv_evac(nxt)
                    s_silu(nxt)
                if cg >= 1:
                    t_x2F(gc - 1)
                    s_x2e(gc - 1)
                t_y2T(gc)
                s_y2f(gc)
                t_o(gc)
                v_ln2(gc)
                if cg + 2 < CPG:
                    x_load(base + cg + 2)
                s_sq2(gc)
                pump()
                v_x2(gc)
                s_x2b(gc)
                if dbg in ("probe", "ffn2"):
                    par = gc % 2
                    t0 = gc * 128
                    for nm, src_ap, ev in (("dq", qbf[par][:], E_qe[gc]),
                                           ("dv", vbf[par][:], E_ve[gc]),
                                           ("dgs", gsb[par][:], E_gs[gc]),
                                           ("dy2f", y2f[par][:], E_y2e[gc]),
                                           ("dx2", facc[:, cg, :], E_x2[gc])):
                        P.op("G", lambda gp, nm=nm, src_ap=src_ap, t0=t0: gp.dma_start(
                            out=dbg_d[nm][t0 : t0 + 128, :], in_=src_ap),
                            [ev], inc=("O", 16))
            t_x2F(base + CPG - 1)
            s_x2e(base + CPG - 1)
            if g == 0:
                conv_pump(NCONV)
                wload_once(0)
                wload_once(1)
                for _ in range(6):
                    pump()
            if dbg == "probe":
                P.op("G", lambda gp: gp.dma_start(out=dtm_d, in_=tmat[:]),
                     [last_texp], inc=("O", 16))

                P.op("G", lambda gp: gp.dma_start(out=dsum_d, in_=sums[:]),
                     [E_x2[base + CPG - 1]], inc=("O", 16))
                for cg in range(CPG):
                    t0 = (base + cg) * 128
                    P.op("G", lambda gp, cg=cg, t0=t0: gp.dma_start(
                        out=y_d[t0 : t0 + 128, :], in_=facc[:, cg, :]),
                        [E_x2[base + cg]], inc=("O", 16))
                continue
            if dbg == "x2":
                for cg in range(CPG):
                    t0 = (base + cg) * 128
                    P.op("G", lambda gp, cg=cg, t0=t0: gp.dma_start(
                        out=y_d[t0 : t0 + 128, :], in_=facc[:, cg, :]),
                        [E_x2[base + cg]], inc=("O", 16))
                continue
            fi0 = (g * NS) * NB
            last_fi = (g * NS + NS) * NB - 1
            ffn_gu(fi0)
            for s_ in range(NS):
                wload_once(g * NS + s_ + 1)
                for b in range(NB):
                    fi = (g * NS + s_) * NB + b
                    pump()
                    ffn_sil(fi)
                    if fi + 1 <= last_fi:
                        ffn_gu(fi + 1)
                    e_last = ffn_down(g, s_, b, fi)
                    if s_ == NS - 1:
                        for cg2, ev2 in e_last.items():
                            ln3_out(base + cg2, ev2)
            E_vffn[g] = P.op("V", lambda v: v.engine_nop(), inc="V")

        ydma_flush()
        total_o = P.cnt.get("O", 0)
        P.op("G", lambda gp: gp.nop(), [("O", total_o)])

        def runner(names):
            def body(eng):
                for (e, fn, ws, inc) in P.ops:
                    if e not in names:
                        continue
                    for (s, v) in ws:
                        eng.wait_ge(SEM[s], v)
                    inst = fn(eng)
                    if inc is not None and inst is not None:
                        inst.then_inc(SEM[inc[0]], inc[1])
            return body

        block.tensor(runner(("T",)))
        block.vector(runner(("V",)))
        block.sync(runner(("Y",)))
        block.gpsimd(runner(("G",)))
        block.scalar(runner(("S", "Q")))

    return nc


_cache = {}


def _get_nc(L, GROUP, SLICE):
    key = (L, GROUP, SLICE)
    if key not in _cache:
        _cache[key] = _build(L, GROUP, SLICE)
    return _cache[key]


def kernel(L0_attn_x, L0_attn_decay, L0_attn_Wq, L0_attn_Wv, L0_attn_Wg, L0_attn_Wo,
           L0_attn_ln_gamma, L0_attn_ln_beta, L0_ffn_w_gate, L0_ffn_w_up, L0_ffn_w_down):
    x = np.ascontiguousarray(np.asarray(L0_attn_x, dtype=np.float32))
    B, L, _ = x.shape
    nc = _get_nc(L, 1024, 256)
    shared = {
        "wqT": np.ascontiguousarray(np.asarray(L0_attn_Wq, np.float32).T),
        "wvT": np.ascontiguousarray(np.asarray(L0_attn_Wv, np.float32).T),
        "wgT": np.ascontiguousarray(np.asarray(L0_attn_Wg, np.float32).T),
        "woT": np.ascontiguousarray(np.asarray(L0_attn_Wo, np.float32).T),
        "dec": np.ascontiguousarray(np.asarray(L0_attn_decay, np.float32)),
        "gam": np.ascontiguousarray(np.asarray(L0_attn_ln_gamma, np.float32)),
        "bet": np.ascontiguousarray(np.asarray(L0_attn_ln_beta, np.float32)),
        "wgq": np.ascontiguousarray(np.asarray(L0_ffn_w_gate, np.int32).T),
        "wuq": np.ascontiguousarray(np.asarray(L0_ffn_w_up, np.int32).T),
        "wdq": np.ascontiguousarray(np.asarray(L0_ffn_w_down, np.int32).T),
    }
    in_maps = []
    for b in range(B):
        m = dict(shared)
        m["x"] = x[b]
        m["xT"] = np.ascontiguousarray(x[b].T)
        in_maps.append(m)
    res = run_bass_kernel_spmd(nc, in_maps, core_ids=list(range(B)))
    return np.stack([np.asarray(r["y"], dtype=np.float32) for r in res.results])
